# version 19
# baseline (speedup 1.0000x reference)
import numpy as np
import ml_dtypes
import concourse.bass as bass
import concourse.mybir as mybir
from concourse.bass_utils import run_bass_kernel_spmd
from contextlib import ExitStack

F32 = mybir.dt.float32
BF16 = mybir.dt.bfloat16
AF = mybir.ActivationFunctionType
ALU = mybir.AluOpType
AX = mybir.AxisListType

D = 1024
S = 8192
HH = 4096
NSWEEP = 8
SLOT_SWEEPS = (8, 6, 4, 2)
LN_EPS = 1e-5
Q_EPS = 1e-6

ENGS = ("y", "g", "s", "v", "t")


class Sem:
    def __init__(self, nc, name, ctx):
        self.h = ctx.enter_context(nc.semaphore(name))
        self.n = 0


class Prog:

    def __init__(self):
        self.tape = {e: [] for e in ENGS}
        self.lastwait = {e: {} for e in ENGS}

    def wait(self, e, sem, val=None):
        if val is None:
            val = sem.n
        if val <= 0:
            return
        lw = self.lastwait[e]
        if lw.get(id(sem), -1) >= val:
            return
        lw[id(sem)] = val
        h = sem.h
        self.tape[e].append(lambda eng: eng.wait_ge(h, val))

    def op(self, e, build, inc=None, k=1, waits=()):
        for w in waits:
            if isinstance(w, tuple):
                self.wait(e, w[0], w[1])
            else:
                self.wait(e, w)
        if inc is not None:
            inc.n += k
            h = inc.h

            def f(eng, build=build, h=h, k=k):
                build(eng).then_inc(h, k)
        else:
            def f(eng, build=build):
                build(eng)
        self.tape[e].append(f)

    def run(self, nc):
        with nc.Block() as blk:
            meths = {"y": blk.sync, "g": blk.gpsimd, "s": blk.scalar,
                     "v": blk.vector, "t": blk.tensor}
            for e in ENGS:
                tape = self.tape[e]
                if tape:
                    meths[e](lambda eng, tape=tape: [f(eng) for f in tape])


def build_nc(dbg=False):
    nc = bass.Bass("TRN2", target_bir_lowering=False, debug=False)

    xk = nc.dram_tensor("xk", [S, D], F32, kind="ExternalInput").ap()
    xq = nc.dram_tensor("xq", [2048, D], F32, kind="ExternalInput").ap()
    wq = nc.dram_tensor("wq", [D, D], F32, kind="ExternalInput").ap()
    wk = nc.dram_tensor("wk", [D, D], F32, kind="ExternalInput").ap()
    wv = nc.dram_tensor("wv", [D, D], F32, kind="ExternalInput").ap()
    wo = nc.dram_tensor("wo", [D, D], F32, kind="ExternalInput").ap()
    wfc = nc.dram_tensor("wfc", [HH, D], F32, kind="ExternalInput").ap()
    wproj = nc.dram_tensor("wproj", [D, HH], F32, kind="ExternalInput").ap()
    lnw_t = nc.dram_tensor("lnw_t", [128, 32], F32, kind="ExternalInput").ap()
    lnb_t = nc.dram_tensor("lnb_t", [128, 32], F32, kind="ExternalInput").ap()
    ident_in = nc.dram_tensor("ident_in", [128, 128], BF16, kind="ExternalInput").ap()
    thr_in = nc.dram_tensor("thr_in", [128, 64], F32, kind="ExternalInput").ap()
    out = nc.dram_tensor("out", [2048, D], F32, kind="ExternalOutput").ap()

    d_wo = nc.dram_tensor("d_wo", [D, D], BF16, kind="Internal").ap()
    d_wk = nc.dram_tensor("d_wk", [128, 8 * 1024], BF16, kind="Internal").ap()
    d_wfcq = nc.dram_tensor("d_wfcq", [128, 8 * HH], BF16, kind="Internal").ap()
    d_wprojq = nc.dram_tensor("d_wprojq", [128, 32 * 1024], BF16, kind="Internal").ap()
    d_xqfm = nc.dram_tensor("d_xqfm", [128, 16384], BF16, kind="Internal").ap()
    d_s = nc.dram_tensor("d_s", [2048, D], BF16, kind="Internal").ap()
    d_z = nc.dram_tensor("d_z", [2048, D], F32, kind="Internal").ap()
    DB = {}
    if dbg:
        for nm, shape, dt in (
                ("qstat", [128, 64], F32), ("qfm", [128, 16384], BF16),
                ("tau", [128, 2], F32), ("cfc", [128, 32], F32),
                ("cb", [128, 8], F32), ("wfcq", [128, 32768], BF16),
                ("wprojq", [128, 32768], BF16), ("Z", [128, 16], F32),
                ("O", [128, 16384], F32), ("ks", [128, 48], F32),
                ("ksb", [128, 8192], BF16), ("vsb", [128, 8192], BF16),
                ("xkfm", [128, 8192], BF16), ("sdump", [2048, D], BF16),
                ("zdump", [2048, D], F32), ("varh", [128, 16], F32),
                ("rh", [128, 16], F32), ("xqfmd", [128, 16384], BF16),
                ("t16", [128, 48], F32), ("Zs", [8, 128, 16], F32),
                ("cbbc", [128, 1024], F32), ("cbfl", [1, 1024], BF16),
                ("Os", [8, 128, 1024], F32), ("hg15", [128, 4096], F32),
                ("hh15", [128, 4096], BF16), ("hfm15", [128, 16384], BF16),
                ("nt", [128, 48], F32)):
            DB[nm] = nc.dram_tensor("dbg_" + nm, shape, dt,
                                    kind="ExternalOutput").ap()

    ctx = ExitStack()
    nc._kb_ctx = ctx
    e = ctx.enter_context

    SM = {n: Sem(nc, n, ctx) for n in (
        "LD", "LK", "CW", "CX", "SQ", "TRD", "TRE", "STA", "KVE", "TSC",
        "SCE", "MSK", "TZ", "TAV", "AVE", "ZAE", "GEN", "GES", "GET",
        "WS", "OUT", "DBG", "LKG0", "LKG1", "LKP0", "LKP1", "WSP0", "WSP1",
        "WSP2", "WSP3", "VF", "SF", "LKJ0", "LKJ1")}
    LD, LK, CW, CX, SQ = SM["LD"], SM["LK"], SM["CW"], SM["CX"], SM["SQ"]
    TRD, TRE, STA, KVE, TSC = SM["TRD"], SM["TRE"], SM["STA"], SM["KVE"], SM["TSC"]
    SCE, MSK, TZ, TAV = SM["SCE"], SM["MSK"], SM["TZ"], SM["TAV"]
    AVE, ZAE, GEN, GES, GET = SM["AVE"], SM["ZAE"], SM["GEN"], SM["GES"], SM["GET"]
    WS, OUT, DBG = SM["WS"], SM["OUT"], SM["DBG"]
    LKG = [SM["LKG0"], SM["LKG1"]]
    LKJ = [SM["LKJ0"], SM["LKJ1"]]
    VF, SF = SM["VF"], SM["SF"]
    LKP = [SM["LKP0"], SM["LKP1"]]
    WSP = [SM["WSP0"], SM["WSP1"]]

    def dump(P, pairs):
        if not dbg:
            return
        for dst, src_ap in pairs:
            P.op("g", lambda g, dst=dst, src_ap=src_ap:
                 g.dma_start(dst, src_ap, allow_slow_non_contiguous=True),
                 inc=DBG, k=16,
                 waits=[(GEN, GEN.n), (GES, GES.n), (GET, GET.n),
                        (TRE, TRE.n), (AVE, AVE.n), (ZAE, ZAE.n),
                        (KVE, KVE.n), (WS, WS.n), (SQ, SQ.n), (CX, CX.n)])

    pA = e(nc.psum_tensor("pA", [128, 512], F32))
    pB = e(nc.psum_tensor("pB", [128, 512], F32))
    pC = e(nc.psum_tensor("pC", [128, 512], F32))
    pD = e(nc.psum_tensor("pD", [128, 512], F32))
    pE = e(nc.psum_tensor("pE", [128, 512], F32))
    pF = e(nc.psum_tensor("pF", [128, 512], F32))
    pz0 = e(nc.psum_tensor("pz0", [128, 16], F32))
    pz1 = e(nc.psum_tensor("pz1", [128, 16], F32))

    bank_free = {}

    def bank_wait(p, el, P):
        bf = bank_free.get(id(p))
        if bf is not None:
            P.wait(el, bf[0], bf[1])

    def bank_used(p, sem):
        bank_free[id(p)] = (sem, sem.n)

    ident = e(nc.sbuf_tensor("ident", [128, 128], BF16))
    ones_col = e(nc.sbuf_tensor("ones_col", [128, 1], BF16))
    ones_row = e(nc.sbuf_tensor("ones_row", [1, 128], BF16))
    ones_col32 = e(nc.sbuf_tensor("ones_col32", [128, 1], F32))
    ones_row32 = e(nc.sbuf_tensor("ones_row32", [1, 128], F32))
    eps5 = e(nc.sbuf_tensor("eps5", [128, 1], F32))
    qsums = e(nc.sbuf_tensor("qsums", [128, 16], F32))
    qsumsq = e(nc.sbuf_tensor("qsumsq", [128, 16], F32))
    munegq = e(nc.sbuf_tensor("munegq", [128, 16], F32))
    munegq_bf = e(nc.sbuf_tensor("munegq_bf", [128, 16], BF16))
    rqs = e(nc.sbuf_tensor("rqs", [128, 16], F32))
    rqhalf = e(nc.sbuf_tensor("rqhalf", [128, 16], F32))
    cfc2_sm = e(nc.sbuf_tensor("cfc2_sm", [128, 32], F32))
    cfc2_bf = e(nc.sbuf_tensor("cfc2_bf", [128, 32], BF16))
    cb8 = e(nc.sbuf_tensor("cb8", [128, 8], F32))
    cb8_bf = e(nc.sbuf_tensor("cb8_bf", [128, 8], BF16))
    varh = e(nc.sbuf_tensor("varh", [128, 16], F32))
    Z_acc = e(nc.sbuf_tensor("Z_acc", [128, 16], F32))
    Zrec = e(nc.sbuf_tensor("Zrec", [128, 16], F32))
    hsums = e(nc.sbuf_tensor("hsums", [128, 32], F32))
    t16a = e(nc.sbuf_tensor("t16a", [128, 16], F32))
    t16b = e(nc.sbuf_tensor("t16b", [128, 16], F32))
    t16c = e(nc.sbuf_tensor("t16c", [128, 16], F32))
    taufp = e(nc.sbuf_tensor("taufp", [128, 1], F32))
    taufn = e(nc.sbuf_tensor("taufn", [128, 1], F32))
    taupp = e(nc.sbuf_tensor("taupp", [128, 1], F32))
    taupn = e(nc.sbuf_tensor("taupn", [128, 1], F32))
    g11 = e(nc.sbuf_tensor("g11", [1, 2], F32))
    lnw_sb = e(nc.sbuf_tensor("lnw_sb", [128, 32], F32))
    lnb_sb = e(nc.sbuf_tensor("lnb_sb", [128, 32], F32))
    lnwh_bf = e(nc.sbuf_tensor("lnwh_bf", [128, 32], BF16))
    lnb_bf = e(nc.sbuf_tensor("lnb_bf", [128, 32], BF16))
    fen_a = e(nc.sbuf_tensor("fen_a", [128, 256], F32))
    fen_c = e(nc.sbuf_tensor("fen_c", [128, 256], F32))
    sfen_a = e(nc.sbuf_tensor("sfen_a", [128, 256], F32))

    def vfence(P):
        P.op("v", lambda v: v.tensor_copy(fen_a[:], fen_c[:]), inc=VF)
        P.wait("v", VF)

    def sfence(P):
        P.op("s", lambda s_: s_.copy(sfen_a[:], fen_c[:]), inc=SF)
        P.wait("s", SF)

    a2 = ExitStack(); nc._kb_a2 = a2; e2 = a2.enter_context
    O_acc = e2(nc.sbuf_tensor("O_acc", [128, 16 * 1024], F32))
    K_sb = e2(nc.sbuf_tensor("K_sb", [128, 8 * 1024], BF16))
    ET = [e2(nc.sbuf_tensor(f"ET{i}", [128, 1024], BF16)) for i in range(4)]

    a1 = ExitStack(); nc._kb_a1 = a1; e1 = a1.enter_context
    Q_fm = e1(nc.sbuf_tensor("Q_fm", [128, 8 * 2048], BF16))
    wkst = [e1(nc.sbuf_tensor(f"wkst{i}", [128, 1024], BF16)) for i in range(2)]
    Wv_sb = e1(nc.sbuf_tensor("Wv_sb", [128, 8 * 1024], BF16))
    V_sb = e1(nc.sbuf_tensor("V_sb", [128, 8 * 1024], BF16))
    Xk_fm = e1(nc.sbuf_tensor("Xk_fm", [128, 8 * 1024], BF16))
    xst = [e1(nc.sbuf_tensor(f"xst{i}", [128, 1024], F32)) for i in range(2)]
    xkbf = [e1(nc.sbuf_tensor(f"xkbf{i}", [128, 1024], BF16)) for i in range(4)]
    iq = e1(nc.sbuf_tensor("iq", [128, 512], F32))
    thr_sb = e1(nc.sbuf_tensor("thr_sb", [128, 64], F32))
    cmp_t = e1(nc.sbuf_tensor("cmp_t", [128, 1024], BF16))
    ksums = e1(nc.sbuf_tensor("ksums", [128, 8], F32))
    ksumsq = e1(nc.sbuf_tensor("ksumsq", [128, 8], F32))
    munegk = e1(nc.sbuf_tensor("munegk", [128, 8], F32))
    munegk_bf = e1(nc.sbuf_tensor("munegk_bf", [128, 8], BF16))
    rks = [e1(nc.sbuf_tensor(f"rks{i}", [128, 8], F32)) for i in range(2)]
    rk32 = [e1(nc.sbuf_tensor(f"rk32{i}", [128, 8], F32)) for i in range(2)]
    mukflat = e1(nc.sbuf_tensor("mukflat", [1, 1024], BF16))
    t8a = e1(nc.sbuf_tensor("t8a", [128, 8], F32))
    t8b = e1(nc.sbuf_tensor("t8b", [128, 8], F32))
    cqrow = e1(nc.sbuf_tensor("cqrow", [1, 1024], BF16))
    ckrow = e1(nc.sbuf_tensor("ckrow", [1, 1024], BF16))
    cvrow = e1(nc.sbuf_tensor("cvrow", [1, 1024], BF16))

    P = Prog()

    def muq_sl(gi):
        t_ = ckrow if gi < 8 else cvrow
        return t_[0:1, 128 * (gi % 8):128 * (gi % 8) + 128]

    def xqfm_sl(c, i):
        col = 512 * c + 128 * i
        return ET[col // 1024][:, col % 1024:col % 1024 + 128]

    P.op("v", lambda v: v.memset(ones_col[:], 1.0))
    P.op("v", lambda v: v.memset(ones_row[:], 1.0))
    P.op("v", lambda v: v.memset(ones_col32[:], 1.0))
    P.op("v", lambda v: v.memset(ones_row32[:], 1.0))
    P.op("v", lambda v: v.memset(fen_c[:], 0.0))
    P.op("v", lambda v: v.memset(eps5[:], LN_EPS), inc=GEN)
    P.op("g", lambda g: g.iota(iq[:], [[1, 512]], channel_multiplier=0,
                               allow_small_or_imprecise_dtypes=True))
    P.op("y", lambda y: y.dma_start(ident[:], ident_in), inc=LD, k=16)
    P.op("y", lambda y: y.dma_start(thr_sb[:], thr_in), inc=LD, k=16)
    P.op("y", lambda y: y.dma_start(lnw_sb[:], lnw_t), inc=LD, k=16)
    P.op("y", lambda y: y.dma_start(lnb_sb[:], lnb_t), inc=LD, k=16)
    P.op("v", lambda v: v.tensor_scalar(lnwh_bf[:], lnw_sb[:], 0.5, None, ALU.mult),
         inc=GEN, waits=[(LD, 64)])
    P.op("v", lambda v: v.tensor_copy(lnb_bf[:], lnb_sb[:]), inc=GEN,
         waits=[(LD, 64)])
    const_v = GEN.n

    for wi, (wsrc, dst) in enumerate([(wq, K_sb), (wv, Wv_sb)]):
        for c in range(8):
            buf = xst[c % 2]
            lp = LKP[c % 2]
            P.op("y", lambda y, wsrc=wsrc, c=c, buf=buf:
                 y.dma_start(buf[:], wsrc[128 * c:128 * (c + 1), :]),
                 inc=lp, k=16, waits=[(CW, max(0, CW.n - 1))])
            P.op("s", lambda s_, buf=buf, dst=dst, c=c:
                 s_.activation(dst[:, 1024 * c:1024 * (c + 1)], buf[:], AF.Copy),
                 inc=CW, waits=[lp])
    for c in range(8):
        buf, kb = xst[c % 2], xkbf[c % 2]
        lp, wp = LKP[c % 2], WSP[c % 2]
        P.op("y", lambda y, c=c, buf=buf:
             y.dma_start(buf[:], wo[128 * c:128 * (c + 1), :]),
             inc=lp, k=16, waits=[(CW, max(0, CW.n - 1))])
        P.op("s", lambda s_, buf=buf, kb=kb:
             s_.activation(kb[:], buf[:], AF.Copy), inc=CW,
             waits=[lp, wp])
        P.op("g", lambda g, kb=kb, c=c:
             g.dma_start(d_wo[128 * c:128 * (c + 1), :], kb[:]),
             inc=wp, k=16, waits=[CW])

    for h in range(2):
        p = [pA, pB][h]
        bank_wait(p, "t", P)
        for c in range(8):
            P.op("t", lambda t, p=p, c=c, h=h:
                 t.matmul(p[0:1, :], ones_col[:],
                          K_sb[:, 1024 * c + 512 * h:1024 * c + 512 * h + 512],
                          start=(c == 0), stop=(c == 7)),
                 inc=(GET if c == 7 else None),
                 waits=([(CW, 8), (GEN, const_v)] if c == 0 else []))
        P.op("v", lambda v, p=p, h=h:
             v.tensor_copy(cqrow[0:1, 512 * h:512 * h + 512], p[0:1, :]),
             inc=GEN, waits=[GET])
        bank_used(p, GEN)

    xqtrd = {}
    for gi in range(16):
        buf, kb = xst[gi % 2], xkbf[gi % 2]
        lp = LKP[gi % 2]
        xw = ([(CW, CW.n), (WSP[0],), (WSP[1],)] if gi < 2 else [])
        xw = [(w[0], w[0].n) if len(w) == 1 else w for w in xw]
        P.op("y", lambda y, gi=gi, buf=buf:
             y.dma_start(buf[:], xq[128 * gi:128 * (gi + 1), :]),
             inc=lp, k=16,
             waits=[(CX, max(0, CX.n - 1)), (SQ, max(0, SQ.n - 1))] + xw)
        P.op("s", lambda s_, buf=buf, kb=kb, gi=gi:
             s_.activation(kb[:], buf[:], AF.Copy, accum_out=qsums[:, gi:gi + 1]),
             inc=CX, waits=[lp, (TRD, xqtrd.get(gi - 2, 0))] + xw)
        P.op("v", lambda v, buf=buf, gi=gi:
             v.scalar_tensor_tensor(buf[:], buf[:], 1.0, buf[:], ALU.mult,
                                    ALU.mult, accum_out=qsumsq[:, gi:gi + 1]),
             inc=SQ, waits=[lp, (CX, CX.n)])
        i = gi % 4
        for half in range(2):
            p = [pC, pD][half]
            bank_wait(p, "t", P)
            for cc in range(4):
                c = 4 * half + cc
                P.op("t", lambda t, p=p, kb=kb, c=c, cc=cc:
                     t.matmul(p[:, 128 * cc:128 * (cc + 1)],
                              kb[:, 128 * c:128 * (c + 1)], ident[:],
                              start=True, stop=True),
                     inc=(TRD if cc == 3 else None),
                     waits=([CX, (LD, 64)] if cc == 0 else []))
            for eh in range(2):
                dst = ET[2 * half + eh]
                P.op("v", lambda v, p=p, dst=dst, eh=eh, i=i:
                     v.tensor_copy(
                         bass.AP(dst, 128 * i, [[1024, 128], [512, 2], [1, 128]]),
                         p[:, 256 * eh:256 * (eh + 1)]),
                     inc=TRE,
                     waits=[TRD] + ([(WS, WS.n)]
                                    if gi % 4 == 0 and eh == 0 else []))
            bank_used(p, TRE)
        xqtrd[gi] = TRD.n
        if gi % 4 == 3:
            gnum = gi // 4
            for tpart in range(4):
                P.op("g", lambda g, gnum=gnum, tpart=tpart:
                     g.dma_start(d_xqfm[:, 4096 * gnum + 1024 * tpart:
                                        4096 * gnum + 1024 * (tpart + 1)],
                                 ET[tpart][:]),
                     inc=WS, k=16, waits=[TRE])

    P.op("v", lambda v: v.tensor_scalar(t16a[:], qsums[:], 1.0 / D, None,
                                        ALU.mult), waits=[(CX, CX.n)])
    P.op("v", lambda v: v.tensor_scalar(t16b[:], qsumsq[:], 1.0 / D, None,
                                        ALU.mult), waits=[(SQ, SQ.n)])
    vfence(P)
    P.op("v", lambda v: v.tensor_tensor(t16c[:], t16a[:], t16a[:], ALU.mult))
    P.op("v", lambda v: v.tensor_scalar(munegq[:], qsums[:], -1.0 / D, None,
                                        ALU.mult))
    vfence(P)
    P.op("v", lambda v: v.tensor_tensor(t16b[:], t16b[:], t16c[:], ALU.subtract))
    P.op("v", lambda v: v.tensor_copy(munegq_bf[:], munegq[:]), inc=GEN)
    vfence(P)
    P.op("s", lambda s_: s_.activation(t16c[:], t16b[:], AF.Ln, bias=eps5[:]),
         waits=[GEN])
    sfence(P)
    P.op("s", lambda s_: s_.activation(rqs[:], t16c[:], AF.Exp, scale=-0.5),
         inc=GES)
    P.op("v", lambda v: v.tensor_scalar(rqhalf[:], rqs[:], 0.5, None, ALU.mult),
         inc=GEN, waits=[GES])
    for i in range(16):
        p = [pE, pF][i % 2]
        bank_wait(p, "t", P)
        P.op("t", lambda t, p=p, i=i:
             t.matmul(p[0:1, 0:128], munegq_bf[:, i:i + 1], ident[:],
                      start=True, stop=True), inc=GET, waits=[GEN])
        P.op("v", lambda v, p=p, i=i:
             v.tensor_copy(muq_sl(i), p[0:1, 0:128]),
             inc=GEN, waits=[GET])
        bank_used(p, GEN)

    for gnum in range(4):
        for tpart in range(4):
            P.op("y", lambda y, gnum=gnum, tpart=tpart:
                 y.dma_start(ET[tpart][:],
                             d_xqfm[:, 4096 * gnum + 1024 * tpart:
                                    4096 * gnum + 1024 * (tpart + 1)]),
                 inc=LK, k=16, waits=[(WS, WS.n), (TRE, TRE.n), (GET, GET.n)])
        for i in range(4):
            gi = 4 * gnum + i
            qtm = xkbf[gi % 2]
            for h in range(2):
                p = [pA, pB][h]
                bank_wait(p, "t", P)
                for c in range(8):
                    P.op("t", lambda t, p=p, c=c, h=h, i=i:
                         t.matmul(p[:], xqfm_sl(c, i),
                                  K_sb[:, 1024 * c + 512 * h:1024 * c + 512 * h + 512],
                                  start=(c == 0), stop=False),
                         waits=([(LK, LK.n), (GEN, GEN.n)] if c == 0 else []))
                P.op("t", lambda t, p=p, gi=gi, h=h:
                     t.matmul(p[:], muq_sl(gi),
                              cqrow[0:1, 512 * h:512 * h + 512],
                              start=False, stop=True), inc=GET)
                P.op("v", lambda v, p=p, qtm=qtm, h=h, gi=gi:
                     v.tensor_scalar(qtm[:, 512 * h:512 * h + 512], p[:],
                                     rqs[:, gi:gi + 1], None, ALU.mult),
                     inc=GEN, waits=[GET])
                bank_used(p, GEN)
            for half in range(2):
                p = [pC, pD][half]
                bank_wait(p, "t", P)
                for kk in range(4):
                    k = 4 * half + kk
                    P.op("t", lambda t, p=p, qtm=qtm, k=k, kk=kk:
                         t.matmul(p[:, 128 * kk:128 * (kk + 1)],
                                  qtm[:, 128 * k:128 * (k + 1)], ident[:],
                                  start=True, stop=True),
                         inc=(TRD if kk == 3 else None),
                         waits=([GEN] if kk == 0 else []))
                k0 = 4 * half
                P.op("v", lambda v, p=p, k0=k0, gi=gi:
                     v.tensor_copy(
                         bass.AP(Q_fm, 2048 * k0 + 128 * gi,
                                 [[16384, 128], [2048, 4], [1, 128]]),
                         p[:]),
                     inc=TRE, waits=[TRD])
                bank_used(p, TRE)
    qproj_t = GET.n
    qproj_trd = TRD.n

    ck_tsc = {}
    for c in range(8):
        buf, kb = xst[c % 2], xkbf[c % 2]
        kw = ([(SQ, SQ.n), (CX, CX.n)] if c < 2 else [])
        lp, wp = LKP[c % 2], WSP[c % 2]
        P.op("y", lambda y, c=c, buf=buf:
             y.dma_start(buf[:], wk[128 * c:128 * (c + 1), :]),
             inc=lp, k=16, waits=[(CW, max(0, CW.n - 1))] + kw)
        P.op("s", lambda s_, buf=buf, kb=kb:
             s_.activation(kb[:], buf[:], AF.Copy), inc=CW,
             waits=[lp, wp, (GET, qproj_t), (TRD, qproj_trd),
                    (TSC, ck_tsc.get(c - 2, 0))])
        for h in range(2):
            p = [pA, pB][h]
            bank_wait(p, "t", P)
            P.op("t", lambda t, p=p, kb=kb, c=c, h=h:
                 t.matmul(p[0:1, :], ones_col[:],
                          kb[:, 512 * h:512 * h + 512],
                          start=(c == 0), stop=(c == 7)),
                 inc=(GET if c == 7 else TSC), waits=[CW, (GEN, const_v)])
            if c == 7:
                P.op("v", lambda v, p=p, h=h:
                     v.tensor_copy(ckrow[0:1, 512 * h:512 * h + 512], p[0:1, :]),
                     inc=GEN, waits=[GET])
                bank_used(p, GEN)
        ck_tsc[c] = TSC.n
        P.op("g", lambda g, kb=kb, c=c:
             g.dma_start(bass.AP(d_wk.tensor, 128 * c,
                                 [[8192, 128], [1024, 8], [1, 128]]),
                         kb[:]),
             inc=wp, k=16, waits=[CW, (GET, GET.n)])
    for h in range(2):
        p = [pC, pD][h]
        bank_wait(p, "t", P)
        for c in range(8):
            P.op("t", lambda t, p=p, c=c, h=h:
                 t.matmul(p[0:1, :], ones_col[:],
                          Wv_sb[:, 1024 * c + 512 * h:1024 * c + 512 * h + 512],
                          start=(c == 0), stop=(c == 7)),
                 inc=(GET if c == 7 else None),
                 waits=([(GET, qproj_t), (GEN, const_v)] if c == 0 else []))
        P.op("v", lambda v, p=p, h=h:
             v.tensor_copy(cvrow[0:1, 512 * h:512 * h + 512], p[0:1, :]),
             inc=GEN, waits=[GET])
        bank_used(p, GEN)

    for _w in (WSP[0], WSP[1], WS):
        P.wait("g", _w)
    dump(P, [(DB.get("qstat"), qsums[:]) if dbg else None] and ([
        (DB["qstat"][0:128, 0:16], qsums[:]),
        (DB["qstat"][0:128, 16:32], qsumsq[:]),
        (DB["qstat"][0:128, 32:48], munegq[:]),
        (DB["qstat"][0:128, 48:64], rqs[:]),
        (DB["qfm"], Q_fm[:]),
        (DB["xqfmd"], d_xqfm),
        (DB["t16"][0:128, 0:16], t16a[:]),
        (DB["t16"][0:128, 16:32], t16b[:]),
        (DB["t16"][0:128, 32:48], t16c[:]),
    ] if dbg else []))
    P.run(nc)
    P = Prog()

    bq = ExitStack(); nc._kb_bq = bq

    def sa_u(u):
        return V_sb[:, 2048 * (u % 4):2048 * (u % 4) + 1024]

    def sb_u(u):
        return V_sb[:, 2048 * (u % 4) + 1024:2048 * (u % 4) + 2048]

    def b_bc_sl(lo, n):
        return Xk_fm[:, lo:lo + n]

    def lw_bc_sl(lo, n):
        return Xk_fm[:, 4096 + lo:4096 + lo + n]

    def evb_sl(eslot):
        return K_sb[:, 512 * eslot:512 * (eslot + 1)]

    def rowbuf_sl(off):
        return K_sb[0:1, 2048 + off:2048 + off + 128]

    gst = [O_acc[:, 8192 + 2048 * k:8192 + 2048 * (k + 1)] for k in range(4)]
    abs_outs = [O_acc[:, 2048 * k:2048 * (k + 1)] for k in range(4)]
    pool = [LKP[0], LKP[1], LK, LD]
    wsem = [WSP[0], WSP[1], SM["WSP2"], SM["WSP3"]]
    tbank = [pE, pF, pC, pD]
    gctr = [0]
    gparts = bq.enter_context(nc.sbuf_tensor("gparts", [128, 64], F32))
    cbparts = bq.enter_context(nc.sbuf_tensor("cbparts", [128, 64], F32))
    cons_hist = {}
    sign_hist = {}
    stt_hist = {}

    for col in range(32):
        for si, (srcsb, dstbc, off) in enumerate(
                ((lnb_bf, b_bc_sl, 0), (lnwh_bf, lw_bc_sl, 128))):
            p = [pE, pF][(2 * col + si) % 2]
            bank_wait(p, "t", P)
            P.op("t", lambda t, p=p, srcsb=srcsb, col=col:
                 t.matmul(p[0:1, 0:128], srcsb[:, col:col + 1], ident[:],
                          start=True, stop=True), inc=GET)
            P.op("v", lambda v, p=p, off=off:
                 v.tensor_copy(rowbuf_sl(off), p[0:1, 0:128]),
                 inc=GEN, waits=[GET])
            bank_used(p, GEN)
            pb = [pC, pD][(2 * col + si) % 2]
            bank_wait(pb, "t", P)
            P.op("t", lambda t, pb=pb, off=off:
                 t.matmul(pb[:, 0:128], ones_row[:], rowbuf_sl(off),
                          start=True, stop=True), inc=GET, waits=[GEN])
            P.op("v", lambda v, pb=pb, dstbc=dstbc, col=col:
                 v.tensor_copy(dstbc(128 * col, 128), pb[:, 0:128]),
                 inc=GEN, waits=[GET])
            bank_used(pb, GEN)
    bc_v = GEN.n

    for i in range(32):
        lsem = pool[i % 4]
        buf = gst[i % 4]
        if i < 16:
            ap = bass.AP(wfc.tensor, 256 * 1024 * i,
                         [[1024, 128], [128 * 1024, 2], [1, 1024]])
        else:
            j = i - 16
            ap = bass.AP(wproj.tensor, HH * 128 * (j // 2) + 2048 * (j % 2),
                         [[HH, 128], [1, 2048]])
        w = cons_hist.get(i % 4)
        P.op("y", lambda y, buf=buf, ap=ap: y.dma_start(buf, ap),
             inc=lsem, k=16, waits=([(GES, w)] if w else []))
        P.op("s", lambda s_, buf=buf, i=i:
             s_.activation(abs_outs[i % 4], buf, AF.Abs,
                           accum_out=gparts[:, i:i + 1]),
             inc=GES, waits=[lsem])
        cons_hist[i % 4] = GES.n
    ges_gamma = GES.n

    def gamma_finalize(col0, dst_p, dst_n, gslot):
        P.op("v", lambda v: v.reduce_sum(t16a[:, gslot:gslot + 1],
                                         gparts[:, col0:col0 + 16], AX.X),
             inc=GEN, waits=[(GES, ges_gamma)])
        p = pA
        bank_wait(p, "t", P)
        P.op("t", lambda t, p=p: t.matmul(p[0:1, 0:1], t16a[:, gslot:gslot + 1],
                                          ones_col32[:], start=True, stop=True),
             inc=GET, waits=[GEN])
        P.op("v", lambda v, p=p: v.tensor_scalar(g11[0:1, gslot:gslot + 1],
                                                 p[0:1, 0:1], 0.5 / (HH * D),
                                                 0.5 * Q_EPS, ALU.mult, ALU.add),
             inc=GEN, waits=[GET])
        bank_used(p, GEN)
        p2 = pB
        bank_wait(p2, "t", P)
        P.op("t", lambda t, p2=p2: t.matmul(p2[:, 0:1], ones_row32[:],
                                            g11[0:1, gslot:gslot + 1],
                                            start=True, stop=True),
             inc=GET, waits=[GEN])
        P.op("v", lambda v, p2=p2: v.tensor_copy(dst_p[:], p2[:, 0:1]),
             waits=[GET])
        P.op("v", lambda v, p2=p2: v.tensor_scalar(dst_n[:], p2[:, 0:1], -1.0,
                                                   None, ALU.mult), inc=GEN)
        bank_used(p2, GEN)

    gamma_finalize(0, taufp, taufn, 0)
    gamma_finalize(16, taupp, taupn, 1)
    tau_v = GEN.n

    for rgrp in range(16):
        lsem = pool[rgrp % 4]
        buf = gst[rgrp % 4]
        w = cons_hist.get(rgrp % 4)
        P.op("y", lambda y, rgrp=rgrp, buf=buf:
             y.dma_start(buf, bass.AP(wfc.tensor, 256 * 1024 * rgrp,
                                      [[1024, 128], [128 * 1024, 2], [1, 1024]])),
             inc=lsem, k=16, waits=([(GES, w)] if w else []))
        for rin in range(2):
            u = 2 * rgrp + rin
            src = buf[:, 1024 * rin:1024 * (rin + 1)]
            P.op("s", lambda s_, src=src, u=u:
                 s_.activation(sa_u(u), src, AF.Sign, bias=taufn[:]),
                 waits=[lsem, (GEN, tau_v), (SQ, stt_hist.get(u - 4, 0))])
            P.op("s", lambda s_, src=src, u=u:
                 s_.activation(sb_u(u), src, AF.Sign, bias=taufp[:]), inc=GES)
            sign_hist[u] = GES.n
            P.op("v", lambda v, rin=rin, u=u:
                 v.scalar_tensor_tensor(xkbf[rin][:], sa_u(u), 1.0, sb_u(u),
                                        ALU.mult, ALU.add,
                                        accum_out=cfc2_sm[:, u:u + 1]),
                 inc=SQ, waits=[(GES, sign_hist[u]), (TRD, TRD.n)]
                               + ([(WSP[rin], WSP[rin].n)] if u < 2 else []))
            stt_hist[u] = SQ.n
        cons_hist[rgrp % 4] = GES.n
        for cp in range(4):
            slot = gctr[0] % 4
            gctr[0] += 1
            p = tbank[slot]
            bank_wait(p, "t", P)
            for bi in range(4):
                ci, rin = 2 * cp + bi // 2, bi % 2
                P.op("t", lambda t, p=p, bi=bi, ci=ci, rin=rin:
                     t.matmul(p[:, 128 * bi:128 * (bi + 1)],
                              xkbf[rin][:, 128 * ci:128 * (ci + 1)], ident[:],
                              start=True, stop=True),
                     inc=(TRD if bi == 3 else None),
                     waits=([SQ] if bi == 0 else []))
            ws = wsem[slot]
            P.op("v", lambda v, p=p, slot=slot:
                 v.tensor_copy(evb_sl(slot), p[:]),
                 inc=TRE, waits=[TRD, (ws, ws.n)])
            bank_used(p, TRE)
            c0 = 2 * cp
            P.op("g", lambda g, c0=c0, rgrp=rgrp, slot=slot:
                 g.dma_start(
                     bass.AP(d_wfcq.tensor, 4096 * c0 + 256 * rgrp,
                             [[8 * HH, 128], [4096, 2], [1, 256]]),
                     evb_sl(slot)),
                 inc=ws, k=16, waits=[TRE])
    P.op("v", lambda v: v.tensor_copy(cfc2_bf[:], cfc2_sm[:]), inc=GEN,
         waits=[(SQ, SQ.n)])

    for lq in range(16):
        rr, cc2 = lq // 2, lq % 2
        lsem = pool[lq % 4]
        buf = gst[lq % 4]
        w = cons_hist.get(lq % 4)
        P.op("y", lambda y, rr=rr, cc2=cc2, buf=buf:
             y.dma_start(buf, bass.AP(wproj.tensor,
                                      HH * 128 * rr + 2048 * cc2,
                                      [[HH, 128], [1, 2048]])),
             inc=lsem, k=16, waits=([(GES, w)] if w else []))
        for half in range(2):
            cc = 2 * cc2 + half
            u = 32 + 4 * rr + cc
            seg = 1024 * cc
            src = buf[:, 1024 * half:1024 * (half + 1)]
            P.op("s", lambda s_, src=src, u=u:
                 s_.activation(sa_u(u), src, AF.Sign, bias=taupn[:]),
                 waits=[lsem, (GEN, bc_v), (SQ, stt_hist.get(u - 4, 0))])
            P.op("s", lambda s_, src=src, u=u:
                 s_.activation(sb_u(u), src, AF.Sign, bias=taupp[:]), inc=GES)
            sign_hist[u] = GES.n
            q2 = xkbf[cc % 2]
            cbc = 8 * rr + 2 * cc
            P.op("v", lambda v, q2=q2, u=u:
                 v.scalar_tensor_tensor(q2[:], sa_u(u), 1.0, sb_u(u),
                                        ALU.mult, ALU.add),
                 waits=[(GES, sign_hist[u]), (TRD, TRD.n)])
            P.op("v", lambda v, u=u, seg=seg, cbc=cbc:
                 v.scalar_tensor_tensor(ET[2][:], sa_u(u), 1.0,
                                        b_bc_sl(seg, 1024), ALU.mult, ALU.mult,
                                        accum_out=cbparts[:, cbc:cbc + 1]))
            P.op("v", lambda v, u=u, seg=seg, cbc=cbc:
                 v.scalar_tensor_tensor(K_sb[:, 4096:5120], sb_u(u), 1.0,
                                        b_bc_sl(seg, 1024), ALU.mult, ALU.mult,
                                        accum_out=cbparts[:, cbc + 1:cbc + 2]))
            vfence(P)
            P.op("v", lambda v, q2=q2, seg=seg:
                 v.tensor_tensor(ET[3][:], q2[:], lw_bc_sl(seg, 1024), ALU.mult),
                 inc=SQ)
            stt_hist[u] = SQ.n
            for hb in range(2):
                slot = gctr[0] % 4
                gctr[0] += 1
                p = tbank[slot]
                bank_wait(p, "t", P)
                for jj4 in range(4):
                    jj = 4 * hb + jj4
                    P.op("t", lambda t, p=p, jj=jj, jj4=jj4:
                         t.matmul(p[:, 128 * jj4:128 * (jj4 + 1)],
                                  ET[3][:, 128 * jj:128 * (jj + 1)], ident[:],
                                  start=True, stop=True),
                         inc=(TRD if jj4 == 3 else None),
                         waits=([SQ] if jj4 == 0 else []))
                ws = wsem[slot]
                P.op("v", lambda v, p=p, slot=slot:
                     v.tensor_copy(evb_sl(slot), p[:]),
                     inc=TRE, waits=[TRD, (ws, ws.n)])
                bank_used(p, TRE)
                jj0 = 4 * hb
                P.op("g", lambda g, cc=cc, jj0=jj0, rr=rr, slot=slot:
                     g.dma_start(
                         bass.AP(d_wprojq.tensor,
                                 1024 * (8 * cc + jj0) + 512 * (rr // 4)
                                 + 128 * (rr % 4),
                                 [[32 * 1024, 128], [1024, 4], [1, 128]]),
                         evb_sl(slot)),
                     inc=ws, k=16, waits=[TRE])
        cons_hist[lq % 4] = GES.n
    for rr in range(8):
        P.op("v", lambda v, rr=rr:
             v.reduce_sum(cb8[:, rr:rr + 1], cbparts[:, 8 * rr:8 * rr + 8], AX.X),
             waits=[(SQ, SQ.n)])
    vfence(P)
    vfence(P)
    P.op("v", lambda v: v.tensor_copy(cb8_bf[:], cb8[:]), inc=GEN)

    for _w in wsem:
        P.wait("g", _w)
    b05w = [(w, w.n) for w in wsem]
    dump(P, ([
        (DB["tau"][0:128, 0:1], taufp[:]), (DB["tau"][0:128, 1:2], taupp[:]),
        (DB["cfc"], cfc2_sm[:]), (DB["cb"], cb8[:]),
        (DB["wfcq"], d_wfcq), (DB["wprojq"], d_wprojq),
    ] if dbg else []))
    P.run(nc)
    P = Prog()
    bq.close()

    def ETsl(tc, lo, n):
        return ET[tc // 2][:, 512 * (tc % 2) + lo: 512 * (tc % 2) + lo + n]

    nslots = [0]

    def xk_prep(s):
        par = s % 2
        for u in range(8):
            buf, kb = xst[u % 2], xkbf[u % 4]
            lp = LKP[u % 2]
            P.op("y", lambda y, s=s, u=u, buf=buf:
                 y.dma_start(buf[:],
                             xk[1024 * s + 128 * u:1024 * s + 128 * (u + 1), :]),
                 inc=lp, k=16,
                 waits=[(CX, max(0, CX.n - 1)), (SQ, max(0, SQ.n - 1))])
            P.op("s", lambda s_, kb=kb, buf=buf, u=u:
                 s_.activation(kb[:], buf[:], AF.Copy,
                               accum_out=ksums[:, u:u + 1]),
                 inc=CX, waits=[lp, (TRD, max(0, TRD.n - 4))])
            P.op("v", lambda v, buf=buf, u=u:
                 v.scalar_tensor_tensor(buf[:], buf[:], 1.0, buf[:], ALU.mult,
                                        ALU.mult, accum_out=ksumsq[:, u:u + 1]),
                 inc=SQ, waits=[lp, (CX, CX.n)])
            for half in range(2):
                p = [pE, pF, pC, pD][(2 * u + half) % 4]
                bank_wait(p, "t", P)
                for cc in range(4):
                    c = 4 * half + cc
                    P.op("t", lambda t, p=p, kb=kb, c=c, cc=cc:
                         t.matmul(p[:, 128 * cc:128 * (cc + 1)],
                                  kb[:, 128 * c:128 * (c + 1)], ident[:],
                                  start=True, stop=True),
                         inc=(TRD if cc == 3 else None),
                         waits=([CX] if cc == 0 else []))
                c0 = 4 * half
                P.op("v", lambda v, p=p, c0=c0, u=u:
                     v.tensor_copy(
                         bass.AP(Xk_fm, 1024 * c0 + 128 * u,
                                 [[8192, 128], [1024, 4], [1, 128]]),
                         p[:]),
                     inc=TRE, waits=[TRD])
                bank_used(p, TRE)
        vfence(P)
        P.op("v", lambda v: v.tensor_scalar(t8a[:], ksums[:], 1.0 / D, None,
                                            ALU.mult), waits=[(CX, CX.n)])
        P.op("v", lambda v: v.tensor_scalar(t8b[:], ksumsq[:], 1.0 / D, None,
                                            ALU.mult), waits=[(SQ, SQ.n)])
        P.op("v", lambda v: v.tensor_scalar(munegk[:], ksums[:], -1.0 / D,
                                            None, ALU.mult))
        vfence(P)
        P.op("v", lambda v: v.tensor_tensor(t8a[:], t8a[:], t8a[:], ALU.mult))
        P.op("v", lambda v: v.tensor_copy(munegk_bf[:], munegk[:]))
        vfence(P)
        P.op("v", lambda v: v.tensor_tensor(t8b[:], t8b[:], t8a[:], ALU.subtract),
             inc=GEN)
        vfence(P)
        P.op("s", lambda s_: s_.activation(t8a[:], t8b[:], AF.Ln, bias=eps5[:]),
             waits=[GEN])
        sfence(P)
        P.op("s", lambda s_, par=par:
             s_.activation(rks[par][:], t8a[:], AF.Exp, scale=-0.5), inc=GES)
        P.op("v", lambda v, par=par:
             v.tensor_scalar(rk32[par][:], rks[par][:], 1.0 / 32, None, ALU.mult),
             inc=GEN, waits=[GES])
        for tc in range(8):
            p = [pE, pF, pC, pD][tc % 4]
            bank_wait(p, "t", P)
            P.op("t", lambda t, p=p, tc=tc:
                 t.matmul(p[0:1, 0:128], munegk_bf[:, tc:tc + 1], ident[:],
                          start=True, stop=True), inc=GET, waits=[GEN])
            P.op("v", lambda v, p=p, tc=tc:
                 v.tensor_copy(mukflat[0:1, 128 * tc:128 * (tc + 1)],
                               p[0:1, 0:128]),
                 inc=(STA if tc == 7 else GEN), waits=[GET])
            bank_used(p, STA if tc == 7 else GEN)

    P.op("v", lambda v: v.memset(O_acc[:], 0.0), inc=GEN)
    P.op("v", lambda v: v.memset(Z_acc[:], 0.0), inc=GEN)
    xk_prep(0)
    kv_banks = [pA, pB, pC, pD]
    wk_hist = [0, 0]

    for s in range(NSWEEP):
        par = s % 2
        active = [j for j in range(4) if SLOT_SWEEPS[j] > s]
        bk = 0
        for k in range(8):
            wb = wkst[k % 2]
            lg = LKG[k % 2]
            P.op("g", lambda g, wb=wb, k=k:
                 g.dma_start(wb[:], d_wk[:, 1024 * k:1024 * (k + 1)]),
                 inc=lg, k=16, waits=[(GET, wk_hist[-2])])
            wk_ld = (lg, lg.n)
            for h in range(2):
                p = kv_banks[bk % 4]; bk += 1
                bank_wait(p, "t", P)
                for c in range(8):
                    P.op("t", lambda t, p=p, c=c, wb=wb, h=h:
                         t.matmul(p[:],
                                  wb[:, 128 * c:128 * (c + 1)],
                                  Xk_fm[:, 1024 * c + 512 * h:1024 * c + 512 * h + 512],
                                  start=(c == 0), stop=False),
                         waits=([(TRE, TRE.n), (STA, s + 1), wk_ld]
                                if c == 0 else []))
                P.op("t", lambda t, p=p, k=k, h=h:
                     t.matmul(p[:], ckrow[0:1, 128 * k:128 * (k + 1)],
                              mukflat[0:1, 512 * h:512 * h + 512],
                              start=False, stop=True), inc=GET)
                P.op("s", lambda s_, p=p, k=k, h=h:
                     s_.activation(K_sb[:, 1024 * k + 512 * h:1024 * k + 512 * h + 512],
                                   p[:], AF.Copy),
                     inc=KVE, waits=[GET, (TAV, 8 * nslots[0])] + b05w)
                bank_used(p, KVE)
            wk_hist.append(GET.n)
        for tc in range(8):
            for h in range(2):
                p = kv_banks[bk % 4]; bk += 1
                bank_wait(p, "t", P)
                for c in range(8):
                    P.op("t", lambda t, p=p, c=c, tc=tc, h=h:
                         t.matmul(p[:],
                                  Xk_fm[:, 1024 * c + 128 * tc:1024 * c + 128 * (tc + 1)],
                                  Wv_sb[:, 1024 * c + 512 * h:1024 * c + 512 * h + 512],
                                  start=(c == 0), stop=False))
                P.op("t", lambda t, p=p, tc=tc, h=h:
                     t.matmul(p[:], mukflat[0:1, 128 * tc:128 * (tc + 1)],
                              cvrow[0:1, 512 * h:512 * h + 512],
                              start=False, stop=True), inc=GET)
                P.op("s", lambda s_, p=p, tc=tc, h=h, par=par:
                     s_.activation(V_sb[:, 1024 * tc + 512 * h:1024 * tc + 512 * h + 512],
                                   p[:], AF.Identity, scale=rks[par][:, tc:tc + 1]),
                     inc=KVE, waits=[GET])
                bank_used(p, KVE)

        for si, j in enumerate(active):
            masked = s >= SLOT_SWEEPS[j] - 2
            for tc in range(8):
                p = [pA, pB][tc % 2]
                bank_wait(p, "t", P)
                for k in range(8):
                    P.op("t", lambda t, p=p, k=k, tc=tc, j=j:
                         t.matmul(p[:],
                                  K_sb[:, 1024 * k + 128 * tc:1024 * k + 128 * (tc + 1)],
                                  Q_fm[:, 2048 * k + 512 * j:2048 * k + 512 * (j + 1)],
                                  start=(k == 0), stop=(k == 7)),
                         inc=(TSC if k == 7 else None),
                         waits=([(KVE, KVE.n)] if k == 0 else []))
                P.op("s", lambda s_, p=p, tc=tc, par=par:
                     s_.activation(ETsl(tc, 0, 512), p[:], AF.Exp,
                                   scale=rk32[par][:, tc:tc + 1]),
                     inc=SCE, waits=[TSC, (TAV, 8 * nslots[0])])
                bank_used(p, SCE)
                if masked:
                    col = 16 * j + 8 * (s - (SLOT_SWEEPS[j] - 2)) + tc
                    cs = cmp_t[:, 512 * (tc % 2):512 * (tc % 2) + 512]
                    P.op("v", lambda v, col=col, cs=cs:
                         v.tensor_single_scalar(cs, iq[:],
                                                thr_sb[:, col:col + 1], ALU.is_ge))
                    vfence(P)
                    P.op("v", lambda v, tc=tc, cs=cs:
                         v.tensor_tensor(ETsl(tc, 0, 512), ETsl(tc, 0, 512),
                                         cs, ALU.mult),
                         inc=MSK, waits=[SCE])
            pz = [pz0, pz1][nslots[0] % 2]
            bank_wait(pz, "t", P)
            for i in range(4):
                for tc in range(8):
                    P.op("t", lambda t, pz=pz, i=i, tc=tc:
                         t.matmul(pz[:, i:i + 1], ETsl(tc, 128 * i, 128),
                                  ones_col[:], start=(tc == 0), stop=(tc == 7)),
                         inc=(TZ if (i == 3 and tc == 7) else None),
                         waits=([(SCE, SCE.n)] + ([(MSK, MSK.n)] if masked else [])
                                if (i == 0 and tc == 0) else []))
            P.op("v", lambda v, pz=pz, j=j:
                 v.tensor_tensor(Z_acc[:, 4 * j:4 * j + 4], pz[:, 0:4],
                                 Z_acc[:, 4 * j:4 * j + 4], ALU.add),
                 inc=ZAE, waits=[TZ])
            bank_used(pz, ZAE)
            for h in range(2):
                for i in range(4):
                    gi = 4 * j + i
                    p = [pC, pD][i % 2]
                    bank_wait(p, "t", P)
                    for tc in range(8):
                        P.op("t", lambda t, p=p, i=i, tc=tc, h=h:
                             t.matmul(p[:], ETsl(tc, 128 * i, 128),
                                      V_sb[:, 1024 * tc + 512 * h:1024 * tc + 512 * h + 512],
                                      start=(tc == 0), stop=(tc == 7)),
                             inc=(TAV if tc == 7 else None))
                    P.op("v", lambda v, p=p, gi=gi, h=h:
                         v.tensor_tensor(
                             O_acc[:, 1024 * gi + 512 * h:1024 * gi + 512 * h + 512],
                             p[:],
                             O_acc[:, 1024 * gi + 512 * h:1024 * gi + 512 * h + 512],
                             ALU.add),
                         inc=AVE, waits=[TAV])
                    bank_used(p, AVE)
            nslots[0] += 1
            if si == 0 and s + 1 < NSWEEP:
                xk_prep(s + 1)
        if dbg:
            P.op("g", lambda g, s=s:
                 g.dma_start(DB["Zs"][s], Z_acc[:]),
                 inc=DBG, k=16, waits=[(ZAE, ZAE.n), (AVE, AVE.n)])
            P.op("g", lambda g, s=s:
                 g.dma_start(DB["Os"][s], O_acc[:, 0:1024]),
                 inc=DBG, k=16)

    dump(P, ([
        (DB["Z"], Z_acc[:]), (DB["O"], O_acc[:]),
        (DB["ks"][0:128, 0:8], ksums[:]), (DB["ks"][0:128, 8:16], ksumsq[:]),
        (DB["ks"][0:128, 16:24], rks[0][:]), (DB["ks"][0:128, 24:32], rks[1][:]),
        (DB["ks"][0:128, 32:40], rk32[0][:]), (DB["ks"][0:128, 40:48], rk32[1][:]),
        (DB["ksb"], K_sb[:]), (DB["vsb"], V_sb[:]), (DB["xkfm"], Xk_fm[:]),
    ] if dbg else []))
    P.run(nc)
    P = Prog()
    a1.close()

    b2s = ExitStack(); nc._kb_b2s = b2s
    sout = [b2s.enter_context(nc.sbuf_tensor(f"sout{i}", [128, 1024], BF16))
            for i in range(2)]
    for c in range(8):
        P.op("y", lambda y, c=c:
             y.dma_start(K_sb[:, 1024 * c:1024 * (c + 1)],
                         d_wo[128 * c:128 * (c + 1), :]),
             inc=LD, k=16)
    wo_ld = LD.n
    P.op("v", lambda v: v.reciprocal(Zrec[:], Z_acc[:]), inc=GEN)
    for gi in range(16):
        obf, ofm = ET[gi % 2], ET[2 + gi % 2]
        P.op("s", lambda s_, obf=obf, gi=gi:
             s_.activation(obf[:], O_acc[:, 1024 * gi:1024 * (gi + 1)], AF.Copy),
             inc=GES, waits=[(GET, GET.n)])
        for half in range(2):
            p = [pE, pF][half]
            bank_wait(p, "t", P)
            for cc in range(4):
                c = 4 * half + cc
                P.op("t", lambda t, p=p, obf=obf, c=c, cc=cc:
                     t.matmul(p[:, 128 * cc:128 * (cc + 1)],
                              obf[:, 128 * c:128 * (c + 1)], ident[:],
                              start=True, stop=True),
                     inc=(TRD if cc == 3 else None),
                     waits=([GES] if cc == 0 else []))
            c0 = 4 * half
            P.op("v", lambda v, p=p, ofm=ofm, c0=c0:
                 v.tensor_copy(ofm[:, 128 * c0:128 * (c0 + 4)], p[:]),
                 inc=TRE, waits=[TRD])
            bank_used(p, TRE)
        for h in range(2):
            p = [pA, pB][h]
            bank_wait(p, "t", P)
            for c in range(8):
                P.op("t", lambda t, p=p, ofm=ofm, c=c, h=h:
                     t.matmul(p[:], ofm[:, 128 * c:128 * (c + 1)],
                              K_sb[:, 1024 * c + 512 * h:1024 * c + 512 * h + 512],
                              start=(c == 0), stop=(c == 7)),
                     inc=(GET if c == 7 else None),
                     waits=([(TRE, TRE.n), (LD, wo_ld)] if c == 0 else []))
            P.op("v", lambda v, p=p, gi=gi, h=h:
                 v.tensor_scalar(sout[gi % 2][:, 512 * h:512 * h + 512], p[:],
                                 Zrec[:, gi:gi + 1], None, ALU.mult),
                 inc=GEN, waits=[GET, (WSP[gi % 2], WSP[gi % 2].n)])
            bank_used(p, GEN)
        P.op("g", lambda g, gi=gi:
             g.dma_start(d_s[128 * gi:128 * (gi + 1), :], sout[gi % 2][:]),
             inc=WSP[gi % 2], k=16, waits=[GEN])

    for _w in (WSP[0], WSP[1]):
        P.wait("g", _w)
    dump(P, ([(DB["sdump"], d_s)] if dbg else []))
    P.run(nc)
    P = Prog()
    b2s.close()
    a2.close()

    f1 = ExitStack(); nc._kb_f1 = f1
    ef = f1.enter_context
    wfc_sb = ef(nc.sbuf_tensor("wfc_sb", [128, 8 * HH], BF16))
    xqg = [ef(nc.sbuf_tensor(f"xqg{i}", [128, 4096], BF16)) for i in range(2)]
    hg = ef(nc.sbuf_tensor("hg", [128, HH], F32))
    hh_bf = ef(nc.sbuf_tensor("hh_bf", [128, HH], BF16))
    sqjunk = ef(nc.sbuf_tensor("sqjunk", [128, HH], BF16))
    hfm = ef(nc.sbuf_tensor("hfm", [128, 32 * 512], BF16))
    zt = [ef(nc.sbuf_tensor(f"zt{i}", [128, 1024], F32)) for i in range(4)]
    wprojst = [ef(nc.sbuf_tensor(f"wpst{i}", [128, 2048], BF16)) for i in range(2)]
    cfc2fl = ef(nc.sbuf_tensor("cfc2fl", [1, HH], BF16))
    muqfl = ef(nc.sbuf_tensor("muqfl", [1, 2048], BF16))
    h1 = ef(nc.sbuf_tensor("h1", [128, 1], F32))
    h2 = ef(nc.sbuf_tensor("h2", [128, 1], F32))
    h3 = ef(nc.sbuf_tensor("h3", [128, 1], F32))

    for i in range(8):
        P.op("y", lambda y, i=i:
             y.dma_start(wfc_sb[:, 4096 * i:4096 * (i + 1)],
                         d_wfcq[:, 4096 * i:4096 * (i + 1)]),
             inc=LD, k=16)
    wfc_ld = LD.n
    for i in range(32):
        p = [pE, pF][i % 2]
        bank_wait(p, "t", P)
        P.op("t", lambda t, p=p, i=i:
             t.matmul(p[0:1, 0:128], cfc2_bf[:, i:i + 1], ident[:],
                      start=True, stop=True), inc=GET)
        P.op("v", lambda v, p=p, i=i:
             v.tensor_copy(cfc2fl[0:1, 128 * i:128 * (i + 1)], p[0:1, 0:128]),
             inc=GEN, waits=[GET])
        bank_used(p, GEN)
    for i in range(16):
        p = [pE, pF][i % 2]
        bank_wait(p, "t", P)
        P.op("t", lambda t, p=p, i=i:
             t.matmul(p[0:1, 0:128], munegq_bf[:, i:i + 1], ident[:],
                      start=True, stop=True), inc=GET)
        P.op("v", lambda v, p=p, i=i:
             v.tensor_copy(muqfl[0:1, 128 * i:128 * (i + 1)], p[0:1, 0:128]),
             inc=GEN, waits=[GET])
        bank_used(p, GEN)
    rows_v, rows_t = GEN.n, GET.n

    fc_t = {}
    jj_hist = [0, 0]
    center_v = {}

    def sched_fc(t):
        gnum, i = t // 4, t % 4
        gi = t
        hcol, scol = 16 * (t % 2), 16 * (t % 2) + 8
        lpg = LKP[gnum % 2]
        if i == 0:
            P.op("y", lambda y, gnum=gnum:
                 y.dma_start(xqg[gnum % 2][:],
                             d_xqfm[:, 4096 * gnum:4096 * (gnum + 1)]),
                 inc=lpg, k=16, waits=[(GET, fc_t.get(gnum - 2, 0))])
        for seg in range(8):
            p = [pA, pB][seg % 2]
            bank_wait(p, "t", P)
            for c in range(8):
                P.op("t", lambda t_, p=p, c=c, seg=seg, i=i, gnum=gnum:
                     t_.matmul(p[:],
                               xqg[gnum % 2][:, 512 * c + 128 * i:
                                             512 * c + 128 * (i + 1)],
                               wfc_sb[:, 4096 * c + 512 * seg:
                                      4096 * c + 512 * (seg + 1)],
                               start=(c == 0), stop=False),
                     waits=([lpg, (LD, wfc_ld), (GEN, rows_v)]
                            if c == 0 else []))
            P.op("t", lambda t_, p=p, gi=gi, seg=seg:
                 t_.matmul(p[:], muqfl[0:1, 128 * gi:128 * (gi + 1)],
                           cfc2fl[0:1, 512 * seg:512 * (seg + 1)],
                           start=False, stop=True), inc=GET)
            P.op("s", lambda s_, p=p, seg=seg, gi=gi, hcol=hcol:
                 s_.activation(hg[:, 512 * seg:512 * (seg + 1)], p[:],
                               AF.Gelu, scale=rqhalf[:, gi:gi + 1],
                               accum_out=hsums[:, hcol + seg:hcol + seg + 1]),
                 inc=GES, waits=[GET, (GEN, center_v.get(t - 1, 0))])
            bank_used(p, GES)
            P.op("v", lambda v, seg=seg, scol=scol:
                 v.scalar_tensor_tensor(sqjunk[:, 512 * seg:512 * (seg + 1)],
                                        hg[:, 512 * seg:512 * (seg + 1)], 1.0,
                                        hg[:, 512 * seg:512 * (seg + 1)],
                                        ALU.mult, ALU.mult,
                                        accum_out=hsums[:, scol + seg:
                                                        scol + seg + 1]),
                 waits=[(GES, GES.n)])
        if i == 3:
            fc_t[gnum] = GET.n

    def sched_stats(t):
        gi = t
        hcol, scol = 16 * (t % 2), 16 * (t % 2) + 8
        P.op("v", lambda v, hcol=hcol:
             v.reduce_sum(h1[:], hsums[:, hcol:hcol + 8], AX.X),
             waits=[(GES, GES.n)])
        vfence(P)
        P.op("v", lambda v: v.tensor_scalar(h2[:], h1[:], -1.0 / HH, None,
                                            ALU.mult))
        vfence(P)
        P.op("v", lambda v:
             v.tensor_scalar(hh_bf[:], hg[:], h2[:], None, ALU.add),
             inc=GEN, waits=[(TRD, TRD.n)])
        center_v[t] = GEN.n
        P.op("v", lambda v, scol=scol:
             v.reduce_sum(h3[:], hsums[:, scol:scol + 8], AX.X))
        vfence(P)
        P.op("v", lambda v: v.tensor_scalar(t16a[:, 0:1], h3[:], 1.0 / HH,
                                            None, ALU.mult))
        P.op("v", lambda v: v.tensor_tensor(t16b[:, 0:1], h2[:], h2[:],
                                            ALU.mult))
        vfence(P)
        P.op("v", lambda v, gi=gi:
             v.tensor_tensor(varh[:, gi:gi + 1], t16a[:, 0:1], t16b[:, 0:1],
                             ALU.subtract), inc=GEN)

    def sched_transposes(t):
        i = t % 4
        for bb in range(8):
            p = [pC, pD][bb % 2]
            bank_wait(p, "t", P)
            for jj4 in range(4):
                jj = 4 * bb + jj4
                P.op("t", lambda t_, p=p, jj=jj, jj4=jj4:
                     t_.matmul(p[:, 128 * jj4:128 * (jj4 + 1)],
                               hh_bf[:, 128 * jj:128 * (jj + 1)], ident[:],
                               start=True, stop=True),
                     inc=(TRD if jj4 == 3 else None),
                     waits=([(GEN, center_v[t])] if jj4 == 0 else []))
            jj0 = 4 * bb
            P.op("v", lambda v, p=p, jj0=jj0, i=i:
                 v.tensor_copy(
                     bass.AP(hfm, 512 * jj0 + 128 * i,
                             [[32 * 512, 128], [512, 4], [1, 128]]),
                     p[:]),
                 inc=TRE, waits=[TRD])
            bank_used(p, TRE)

    def sched_proj(gnum):
        for h in range(2):
            pbanks = [pA, pB, pE, pF]
            for p in pbanks:
                bank_wait(p, "t", P)
            for jg in range(8):
                wb = wprojst[jg % 2]
                lgj = LKJ[jg % 2]
                P.op("y", lambda y, wb=wb, jg=jg, h=h:
                     y.dma_start(wb[:],
                                  bass.AP(d_wprojq.tensor,
                                          1024 * 4 * jg + 512 * h,
                                          [[32 * 1024, 128], [1024, 4], [1, 512]])),
                     inc=lgj, k=16, waits=[(GET, jj_hist[-2])])
                for jl in range(4):
                    jjj = 4 * jg + jl
                    for i in range(4):
                        P.op("t", lambda t_, p=pbanks[i], wb=wb, jjj=jjj, jl=jl, i=i:
                             t_.matmul(p[:],
                                      hfm[:, 512 * jjj + 128 * i:
                                          512 * jjj + 128 * (i + 1)],
                                      wb[:, 512 * jl:512 * (jl + 1)],
                                      start=(jjj == 0), stop=(jjj == 31)),
                             inc=(GET if i == 3 and jl == 3 else None),
                             waits=(([lgj] + ([(TRE, TRE.n)] if jjj == 0 else []))
                                    if i == 0 and jl == 0 else []))
                jj_hist.append(GET.n)
            for i in range(4):
                P.op("s", lambda s_, p=pbanks[i], i=i, h=h:
                     s_.activation(zt[i][:, 512 * h:512 * h + 512], p[:], AF.Copy),
                     inc=GES, waits=[(GET, GET.n), (WS, WS.n)])
                bank_used(pbanks[i], GES)
        for i in range(4):
            gi = 4 * gnum + i
            P.op("g", lambda g, i=i, gi=gi:
                 g.dma_start(d_z[128 * gi:128 * (gi + 1), :], zt[i][:]),
                 inc=WS, k=16, waits=[(GES, GES.n)])

    for t in range(17):
        if t < 16:
            sched_fc(t)
        if t > 0:
            sched_transposes(t - 1)
        if t < 16:
            sched_stats(t)
        if t > 0 and t % 4 == 0:
            sched_proj(t // 4 - 1)

    P.wait("g", WS)
    dump(P, ([(DB["zdump"], d_z), (DB["varh"], varh[:]),
              (DB["hg15"], hg[:]), (DB["hh15"], hh_bf[:]),
              (DB["hfm15"], hfm[:])] if dbg else []))
    P.run(nc)
    P = Prog()

    rh = ef(nc.sbuf_tensor("rh", [128, 16], F32))
    cb_bc = ef(nc.sbuf_tensor("cb_bc", [128, 1024], F32))
    outb = [ef(nc.sbuf_tensor(f"outb{i}", [128, 1024], F32)) for i in range(2)]

    def cb_fl_sl(i):
        return sqjunk[0:1, 128 * i:128 * (i + 1)]

    zin = [zt[0][:], zt[1][:]]
    zbuf = [zt[2][:], zt[3][:]]
    sin = [hh_bf[:, 0:1024], hh_bf[:, 1024:2048]]
    gbuf = [hh_bf[:, 2048:3072], hh_bf[:, 3072:4096]]
    xin = [hg[:, 0:1024], hg[:, 1024:2048]]
    tbuf = [hg[:, 2048:3072], hg[:, 3072:4096]]

    P.op("s", lambda s_: s_.activation(t16a[:], varh[:], AF.Sqrt, bias=eps5[:]),
         inc=GES)
    P.op("v", lambda v: v.reciprocal(t16b[:], t16a[:]), waits=[(GES, GES.n)])
    P.op("v", lambda v: v.tensor_scalar(t16c[:], varh[:], 1.0, LN_EPS, ALU.mult,
                                        ALU.add))
    vfence(P)
    P.op("v", lambda v: v.tensor_tensor(rh[:], t16b[:], t16b[:], ALU.mult))
    vfence(P)
    P.op("v", lambda v: v.tensor_tensor(rh[:], rh[:], t16c[:], ALU.mult))
    vfence(P)
    P.op("v", lambda v: v.tensor_scalar(rh[:], rh[:], -0.5, 1.5, ALU.mult,
                                        ALU.add))
    vfence(P)
    P.op("v", lambda v: v.tensor_tensor(rh[:], rh[:], t16b[:], ALU.mult), inc=GEN)

    for i in range(8):
        p = [pE, pF][i % 2]
        bank_wait(p, "t", P)
        P.op("t", lambda t, p=p, i=i:
             t.matmul(p[0:1, 0:128], cb8_bf[:, i:i + 1], ident[:],
                      start=True, stop=True), inc=GET)
        P.op("v", lambda v, p=p, i=i:
             v.tensor_copy(cb_fl_sl(i), p[0:1, 0:128]),
             inc=GEN, waits=[GET])
        bank_used(p, GEN)
    for i in range(8):
        p = [pA, pB][i % 2]
        bank_wait(p, "t", P)
        P.op("t", lambda t, p=p, i=i:
             t.matmul(p[:, 0:128], ones_row[:], cb_fl_sl(i),
                      start=True, stop=True), inc=GET, waits=[(GEN, GEN.n)])
        P.op("v", lambda v, p=p, i=i:
             v.tensor_scalar(cb_bc[:, 128 * i:128 * (i + 1)], p[:, 0:128],
                             0.5, None, ALU.mult),
             inc=GEN, waits=[GET])
        bank_used(p, GEN)
    prep_v = GEN.n

    tile_v = {}
    for gi in range(16):
        par = gi % 2
        lp = LKP[par]
        P.op("y", lambda y, gi=gi, par=par:
             y.dma_start(zin[par], d_z[128 * gi:128 * (gi + 1), :]),
             inc=lp, k=16, waits=[(GEN, tile_v.get(gi - 2, 0))])
        P.op("s", lambda s_, gi=gi, par=par:
             s_.dma_start(sin[par], d_s[128 * gi:128 * (gi + 1), :]),
             inc=lp, k=16, waits=[(GEN, tile_v.get(gi - 2, 0))])
        P.op("y", lambda y, gi=gi, par=par:
             y.dma_start(xin[par], xq[128 * gi:128 * (gi + 1), :]),
             inc=lp, k=16)
        P.op("v", lambda v, gi=gi, par=par:
             v.scalar_tensor_tensor(zbuf[par], zin[par], rh[:, gi:gi + 1],
                                    cb_bc[:], ALU.mult, ALU.add),
             inc=GEN, waits=[(lp, lp.n), (GEN, prep_v), (GES, GES.n)])
        P.op("s", lambda s_, par=par:
             s_.activation(gbuf[par], zbuf[par], AF.Silu),
             inc=GES, waits=[(GEN, GEN.n)])
        P.op("v", lambda v, par=par:
             v.tensor_tensor(tbuf[par], gbuf[par], sin[par], ALU.mult),
             inc=GEN, waits=[(GES, GES.n)])
        vfence(P)
        P.op("v", lambda v, par=par:
             v.tensor_tensor(outb[par][:], tbuf[par], xin[par], ALU.add),
             inc=GEN, waits=[(OUT, OUT.n)])
        tile_v[gi] = GEN.n
        P.op("g", lambda g, gi=gi, par=par:
             g.dma_start(out[128 * gi:128 * (gi + 1), :], outb[par][:]),
             inc=OUT, k=16, waits=[(GEN, GEN.n)])

    nout = OUT.n
    ndbg = DBG.n
    P.op("g", lambda g: g.wait_ge(OUT.h, nout))
    if ndbg:
        P.op("g", lambda g: g.wait_ge(DBG.h, ndbg))
    dump(P, ([(DB["rh"], rh[:]),
              (DB["cbbc"], cb_bc[:]),
              (DB["cbfl"], sqjunk[0:1, 0:1024]),
              (DB["nt"][0:128, 0:16], t16a[:]),
              (DB["nt"][0:128, 16:32], t16b[:]),
              (DB["nt"][0:128, 32:48], t16c[:])] if dbg else []))
    P.run(nc)

    f1.close()
    ctx.close()
    return nc


def _pos(m):
    return [12 + m, 8 + m, 4 + m, m]


def _host_inputs(x, Wq, Wk, Wv, Wo, W_fc, ln_w, ln_b, W_proj):
    bf = ml_dtypes.bfloat16
    ident = np.eye(128, dtype=np.float32).astype(bf)
    lnw_t = np.ascontiguousarray(ln_w.reshape(32, 128).T)
    lnb_t = np.ascontiguousarray(ln_b.reshape(32, 128).T)
    shared = {"wq": Wq, "wk": Wk, "wv": Wv, "wo": Wo, "wfc": W_fc,
              "wproj": W_proj, "lnw_t": lnw_t, "lnb_t": lnb_t,
              "ident_in": ident}
    in_maps = []
    p_arr = np.arange(128)
    for c in range(8):
        b, m = c // 4, c % 4
        pos = _pos(m)
        xkc = np.ascontiguousarray(x[b])
        xqc = np.ascontiguousarray(
            np.concatenate([xkc[512 * p:512 * (p + 1)] for p in pos], axis=0))
        thr = np.zeros((128, 64), dtype=np.float32)
        for j in range(4):
            for srel in range(2):
                s_abs = SLOT_SWEEPS[j] - 2 + srel
                for tc in range(8):
                    col = 16 * j + 8 * srel + tc
                    thr[:, col] = (1024 * s_abs + 128 * tc + p_arr
                                   - 512 * pos[j])
        im = dict(shared)
        im["xk"] = xkc
        im["xq"] = xqc
        im["thr_in"] = thr
        in_maps.append(im)
    return in_maps


def kernel(L0_attn_x, L0_attn_Wq, L0_attn_Wk, L0_attn_Wv, L0_attn_Wo,
           L0_ffn_W_fc, L0_ffn_ln_w, L0_ffn_ln_b, L0_ffn_W_proj):
    x = np.asarray(L0_attn_x, dtype=np.float32)
    in_maps = _host_inputs(
        x,
        np.ascontiguousarray(np.asarray(L0_attn_Wq, np.float32)),
        np.ascontiguousarray(np.asarray(L0_attn_Wk, np.float32)),
        np.ascontiguousarray(np.asarray(L0_attn_Wv, np.float32)),
        np.ascontiguousarray(np.asarray(L0_attn_Wo, np.float32)),
        np.ascontiguousarray(np.asarray(L0_ffn_W_fc, np.float32)),
        np.ascontiguousarray(np.asarray(L0_ffn_ln_w, np.float32)),
        np.ascontiguousarray(np.asarray(L0_ffn_ln_b, np.float32)),
        np.ascontiguousarray(np.asarray(L0_ffn_W_proj, np.float32)))
    nc = build_nc()
    res = run_bass_kernel_spmd(nc, in_maps, core_ids=list(range(8)))
    outa = np.empty((2, 8192, 1024), np.float32)
    for c in range(8):
        b, m = c // 4, c % 4
        o = res.results[c]["out"]
        for j, p in enumerate(_pos(m)):
            outa[b, 512 * p:512 * (p + 1)] = o[512 * j:512 * (j + 1)]
    return outa


# version 20 (final)
# speedup vs baseline: 1.0074x; 1.0074x over previous
import numpy as np
import ml_dtypes
import concourse.bass as bass
import concourse.mybir as mybir
from concourse.bass_utils import run_bass_kernel_spmd
from contextlib import ExitStack

F32 = mybir.dt.float32
BF16 = mybir.dt.bfloat16
AF = mybir.ActivationFunctionType
ALU = mybir.AluOpType
AX = mybir.AxisListType

D = 1024
S = 8192
HH = 4096
NSWEEP = 8
SLOT_SWEEPS = (8, 6, 4, 2)
LN_EPS = 1e-5
Q_EPS = 1e-6

ENGS = ("y", "g", "s", "v", "t")


class Sem:
    def __init__(self, nc, name, ctx):
        self.h = ctx.enter_context(nc.semaphore(name))
        self.n = 0


class Prog:

    def __init__(self):
        self.tape = {e: [] for e in ENGS}
        self.lastwait = {e: {} for e in ENGS}

    def wait(self, e, sem, val=None):
        if val is None:
            val = sem.n
        if val <= 0:
            return
        lw = self.lastwait[e]
        if lw.get(id(sem), -1) >= val:
            return
        lw[id(sem)] = val
        h = sem.h
        self.tape[e].append(lambda eng: eng.wait_ge(h, val))

    def op(self, e, build, inc=None, k=1, waits=()):
        for w in waits:
            if isinstance(w, tuple):
                self.wait(e, w[0], w[1])
            else:
                self.wait(e, w)
        if inc is not None:
            inc.n += k
            h = inc.h

            def f(eng, build=build, h=h, k=k):
                build(eng).then_inc(h, k)
        else:
            def f(eng, build=build):
                build(eng)
        self.tape[e].append(f)

    def run(self, nc):
        with nc.Block() as blk:
            meths = {"y": blk.sync, "g": blk.gpsimd, "s": blk.scalar,
                     "v": blk.vector, "t": blk.tensor}
            for e in ENGS:
                tape = self.tape[e]
                if tape:
                    meths[e](lambda eng, tape=tape: [f(eng) for f in tape])


def build_nc(dbg=False):
    nc = bass.Bass("TRN2", target_bir_lowering=False, debug=False)

    xk = nc.dram_tensor("xk", [S, D], F32, kind="ExternalInput").ap()
    xq = nc.dram_tensor("xq", [2048, D], F32, kind="ExternalInput").ap()
    wq = nc.dram_tensor("wq", [D, D], F32, kind="ExternalInput").ap()
    wk = nc.dram_tensor("wk", [D, D], F32, kind="ExternalInput").ap()
    wv = nc.dram_tensor("wv", [D, D], F32, kind="ExternalInput").ap()
    wo = nc.dram_tensor("wo", [D, D], F32, kind="ExternalInput").ap()
    wfc = nc.dram_tensor("wfc", [HH, D], F32, kind="ExternalInput").ap()
    wproj = nc.dram_tensor("wproj", [D, HH], F32, kind="ExternalInput").ap()
    lnw_t = nc.dram_tensor("lnw_t", [128, 32], F32, kind="ExternalInput").ap()
    lnb_t = nc.dram_tensor("lnb_t", [128, 32], F32, kind="ExternalInput").ap()
    ident_in = nc.dram_tensor("ident_in", [128, 128], BF16, kind="ExternalInput").ap()
    thr_in = nc.dram_tensor("thr_in", [128, 64], F32, kind="ExternalInput").ap()
    out = nc.dram_tensor("out", [2048, D], F32, kind="ExternalOutput").ap()

    d_wo = nc.dram_tensor("d_wo", [D, D], BF16, kind="Internal").ap()
    d_wk = nc.dram_tensor("d_wk", [128, 8 * 1024], BF16, kind="Internal").ap()
    d_wfcq = nc.dram_tensor("d_wfcq", [128, 8 * HH], BF16, kind="Internal").ap()
    d_wprojq = nc.dram_tensor("d_wprojq", [128, 32 * 1024], BF16, kind="Internal").ap()
    d_xqfm = nc.dram_tensor("d_xqfm", [128, 16384], BF16, kind="Internal").ap()
    d_s = nc.dram_tensor("d_s", [2048, D], BF16, kind="Internal").ap()
    d_z = nc.dram_tensor("d_z", [2048, D], F32, kind="Internal").ap()
    DB = {}
    if dbg:
        for nm, shape, dt in (
                ("qstat", [128, 64], F32), ("qfm", [128, 16384], BF16),
                ("tau", [128, 2], F32), ("cfc", [128, 32], F32),
                ("cb", [128, 8], F32), ("wfcq", [128, 32768], BF16),
                ("wprojq", [128, 32768], BF16), ("Z", [128, 16], F32),
                ("O", [128, 16384], F32), ("ks", [128, 48], F32),
                ("ksb", [128, 8192], BF16), ("vsb", [128, 8192], BF16),
                ("xkfm", [128, 8192], BF16), ("sdump", [2048, D], BF16),
                ("zdump", [2048, D], F32), ("varh", [128, 16], F32),
                ("rh", [128, 16], F32), ("xqfmd", [128, 16384], BF16),
                ("t16", [128, 48], F32), ("Zs", [8, 128, 16], F32),
                ("cbbc", [128, 1024], F32), ("cbfl", [1, 1024], BF16),
                ("Os", [8, 128, 1024], F32), ("hg15", [128, 4096], F32),
                ("hh15", [128, 4096], BF16), ("hfm15", [128, 16384], BF16),
                ("nt", [128, 48], F32)):
            DB[nm] = nc.dram_tensor("dbg_" + nm, shape, dt,
                                    kind="ExternalOutput").ap()

    ctx = ExitStack()
    nc._kb_ctx = ctx
    e = ctx.enter_context

    SM = {n: Sem(nc, n, ctx) for n in (
        "LD", "LK", "CW", "CX", "SQ", "TRD", "TRE", "STA", "KVE", "TSC",
        "SCE", "MSK", "TZ", "TAV", "AVE", "ZAE", "GEN", "GES", "GET",
        "WS", "OUT", "DBG", "LKG0", "LKG1", "LKP0", "LKP1", "WSP0", "WSP1",
        "WSP2", "WSP3", "VF", "SF", "LKJ0", "LKJ1")}
    LD, LK, CW, CX, SQ = SM["LD"], SM["LK"], SM["CW"], SM["CX"], SM["SQ"]
    TRD, TRE, STA, KVE, TSC = SM["TRD"], SM["TRE"], SM["STA"], SM["KVE"], SM["TSC"]
    SCE, MSK, TZ, TAV = SM["SCE"], SM["MSK"], SM["TZ"], SM["TAV"]
    AVE, ZAE, GEN, GES, GET = SM["AVE"], SM["ZAE"], SM["GEN"], SM["GES"], SM["GET"]
    WS, OUT, DBG = SM["WS"], SM["OUT"], SM["DBG"]
    LKG = [SM["LKG0"], SM["LKG1"]]
    LKJ = [SM["LKJ0"], SM["LKJ1"]]
    VF, SF = SM["VF"], SM["SF"]
    LKP = [SM["LKP0"], SM["LKP1"]]
    WSP = [SM["WSP0"], SM["WSP1"]]

    def dump(P, pairs):
        if not dbg:
            return
        for dst, src_ap in pairs:
            P.op("g", lambda g, dst=dst, src_ap=src_ap:
                 g.dma_start(dst, src_ap, allow_slow_non_contiguous=True),
                 inc=DBG, k=16,
                 waits=[(GEN, GEN.n), (GES, GES.n), (GET, GET.n),
                        (TRE, TRE.n), (AVE, AVE.n), (ZAE, ZAE.n),
                        (KVE, KVE.n), (WS, WS.n), (SQ, SQ.n), (CX, CX.n)])

    pA = e(nc.psum_tensor("pA", [128, 512], F32))
    pB = e(nc.psum_tensor("pB", [128, 512], F32))
    pC = e(nc.psum_tensor("pC", [128, 512], F32))
    pD = e(nc.psum_tensor("pD", [128, 512], F32))
    pE = e(nc.psum_tensor("pE", [128, 512], F32))
    pF = e(nc.psum_tensor("pF", [128, 512], F32))
    pz0 = e(nc.psum_tensor("pz0", [128, 16], F32))
    pz1 = e(nc.psum_tensor("pz1", [128, 16], F32))

    bank_free = {}

    def bank_wait(p, el, P):
        bf = bank_free.get(id(p))
        if bf is not None:
            P.wait(el, bf[0], bf[1])

    def bank_used(p, sem):
        bank_free[id(p)] = (sem, sem.n)

    ident = e(nc.sbuf_tensor("ident", [128, 128], BF16))
    ones_col = e(nc.sbuf_tensor("ones_col", [128, 1], BF16))
    ones_row = e(nc.sbuf_tensor("ones_row", [1, 128], BF16))
    ones_col32 = e(nc.sbuf_tensor("ones_col32", [128, 1], F32))
    ones_row32 = e(nc.sbuf_tensor("ones_row32", [1, 128], F32))
    eps5 = e(nc.sbuf_tensor("eps5", [128, 1], F32))
    qsums = e(nc.sbuf_tensor("qsums", [128, 16], F32))
    qsumsq = e(nc.sbuf_tensor("qsumsq", [128, 16], F32))
    munegq = e(nc.sbuf_tensor("munegq", [128, 16], F32))
    munegq_bf = e(nc.sbuf_tensor("munegq_bf", [128, 16], BF16))
    rqs = e(nc.sbuf_tensor("rqs", [128, 16], F32))
    rqhalf = e(nc.sbuf_tensor("rqhalf", [128, 16], F32))
    cfc2_sm = e(nc.sbuf_tensor("cfc2_sm", [128, 32], F32))
    cfc2_bf = e(nc.sbuf_tensor("cfc2_bf", [128, 32], BF16))
    cb8 = e(nc.sbuf_tensor("cb8", [128, 8], F32))
    cb8_bf = e(nc.sbuf_tensor("cb8_bf", [128, 8], BF16))
    varh = e(nc.sbuf_tensor("varh", [128, 16], F32))
    Z_acc = e(nc.sbuf_tensor("Z_acc", [128, 16], F32))
    Zrec = e(nc.sbuf_tensor("Zrec", [128, 16], F32))
    hsums = e(nc.sbuf_tensor("hsums", [128, 32], F32))
    t16a = e(nc.sbuf_tensor("t16a", [128, 16], F32))
    t16b = e(nc.sbuf_tensor("t16b", [128, 16], F32))
    t16c = e(nc.sbuf_tensor("t16c", [128, 16], F32))
    taufp = e(nc.sbuf_tensor("taufp", [128, 1], F32))
    taufn = e(nc.sbuf_tensor("taufn", [128, 1], F32))
    taupp = e(nc.sbuf_tensor("taupp", [128, 1], F32))
    taupn = e(nc.sbuf_tensor("taupn", [128, 1], F32))
    g11 = e(nc.sbuf_tensor("g11", [1, 2], F32))
    lnw_sb = e(nc.sbuf_tensor("lnw_sb", [128, 32], F32))
    lnb_sb = e(nc.sbuf_tensor("lnb_sb", [128, 32], F32))
    lnwh_bf = e(nc.sbuf_tensor("lnwh_bf", [128, 32], BF16))
    lnb_bf = e(nc.sbuf_tensor("lnb_bf", [128, 32], BF16))
    fen_a = e(nc.sbuf_tensor("fen_a", [128, 256], F32))
    fen_c = e(nc.sbuf_tensor("fen_c", [128, 256], F32))
    sfen_a = e(nc.sbuf_tensor("sfen_a", [128, 256], F32))

    def vfence(P):
        P.op("v", lambda v: v.tensor_copy(fen_a[:], fen_c[:]), inc=VF)
        P.wait("v", VF)

    def sfence(P):
        P.op("s", lambda s_: s_.copy(sfen_a[:], fen_c[:]), inc=SF)
        P.wait("s", SF)

    a2 = ExitStack(); nc._kb_a2 = a2; e2 = a2.enter_context
    O_acc = e2(nc.sbuf_tensor("O_acc", [128, 16 * 1024], F32))
    K_sb = e2(nc.sbuf_tensor("K_sb", [128, 8 * 1024], BF16))
    ET = [e2(nc.sbuf_tensor(f"ET{i}", [128, 1024], BF16)) for i in range(4)]

    a1 = ExitStack(); nc._kb_a1 = a1; e1 = a1.enter_context
    Q_fm = e1(nc.sbuf_tensor("Q_fm", [128, 8 * 2048], BF16))
    wkst = [e1(nc.sbuf_tensor(f"wkst{i}", [128, 1024], BF16)) for i in range(2)]
    Wv_sb = e1(nc.sbuf_tensor("Wv_sb", [128, 8 * 1024], BF16))
    V_sb = e1(nc.sbuf_tensor("V_sb", [128, 8 * 1024], BF16))
    Xk_fm = e1(nc.sbuf_tensor("Xk_fm", [128, 8 * 1024], BF16))
    xst = [e1(nc.sbuf_tensor(f"xst{i}", [128, 1024], F32)) for i in range(2)]
    xkbf = [e1(nc.sbuf_tensor(f"xkbf{i}", [128, 1024], BF16)) for i in range(4)]
    iq = e1(nc.sbuf_tensor("iq", [128, 512], F32))
    thr_sb = e1(nc.sbuf_tensor("thr_sb", [128, 64], F32))
    cmp_t = e1(nc.sbuf_tensor("cmp_t", [128, 1024], BF16))
    ksums = e1(nc.sbuf_tensor("ksums", [128, 8], F32))
    ksumsq = e1(nc.sbuf_tensor("ksumsq", [128, 8], F32))
    munegk = e1(nc.sbuf_tensor("munegk", [128, 8], F32))
    munegk_bf = e1(nc.sbuf_tensor("munegk_bf", [128, 8], BF16))
    rks = [e1(nc.sbuf_tensor(f"rks{i}", [128, 8], F32)) for i in range(2)]
    rk32 = [e1(nc.sbuf_tensor(f"rk32{i}", [128, 8], F32)) for i in range(2)]
    mukflat = e1(nc.sbuf_tensor("mukflat", [1, 1024], BF16))
    t8a = e1(nc.sbuf_tensor("t8a", [128, 8], F32))
    t8b = e1(nc.sbuf_tensor("t8b", [128, 8], F32))
    cqrow = e1(nc.sbuf_tensor("cqrow", [1, 1024], BF16))
    ckrow = e1(nc.sbuf_tensor("ckrow", [1, 1024], BF16))
    cvrow = e1(nc.sbuf_tensor("cvrow", [1, 1024], BF16))

    P = Prog()

    def muq_sl(gi):
        t_ = ckrow if gi < 8 else cvrow
        return t_[0:1, 128 * (gi % 8):128 * (gi % 8) + 128]

    def xqfm_sl(c, i):
        col = 512 * c + 128 * i
        return ET[col // 1024][:, col % 1024:col % 1024 + 128]

    P.op("v", lambda v: v.memset(ones_col[:], 1.0))
    P.op("v", lambda v: v.memset(ones_row[:], 1.0))
    P.op("v", lambda v: v.memset(ones_col32[:], 1.0))
    P.op("v", lambda v: v.memset(ones_row32[:], 1.0))
    P.op("v", lambda v: v.memset(fen_c[:], 0.0))
    P.op("v", lambda v: v.memset(eps5[:], LN_EPS), inc=GEN)
    P.op("g", lambda g: g.iota(iq[:], [[1, 512]], channel_multiplier=0,
                               allow_small_or_imprecise_dtypes=True))
    P.op("y", lambda y: y.dma_start(ident[:], ident_in), inc=LD, k=16)
    P.op("y", lambda y: y.dma_start(thr_sb[:], thr_in), inc=LD, k=16)
    P.op("y", lambda y: y.dma_start(lnw_sb[:], lnw_t), inc=LD, k=16)
    P.op("y", lambda y: y.dma_start(lnb_sb[:], lnb_t), inc=LD, k=16)
    P.op("v", lambda v: v.tensor_scalar(lnwh_bf[:], lnw_sb[:], 0.5, None, ALU.mult),
         inc=GEN, waits=[(LD, 64)])
    P.op("v", lambda v: v.tensor_copy(lnb_bf[:], lnb_sb[:]), inc=GEN,
         waits=[(LD, 64)])
    const_v = GEN.n

    for wi, (wsrc, dst) in enumerate([(wq, K_sb), (wv, Wv_sb)]):
        for c in range(8):
            buf = xst[c % 2]
            lp = LKP[c % 2]
            P.op("y", lambda y, wsrc=wsrc, c=c, buf=buf:
                 y.dma_start(buf[:], wsrc[128 * c:128 * (c + 1), :]),
                 inc=lp, k=16, waits=[(CW, max(0, CW.n - 1))])
            P.op("s", lambda s_, buf=buf, dst=dst, c=c:
                 s_.activation(dst[:, 1024 * c:1024 * (c + 1)], buf[:], AF.Copy),
                 inc=CW, waits=[lp])
    for c in range(8):
        buf, kb = xst[c % 2], xkbf[c % 2]
        lp, wp = LKP[c % 2], WSP[c % 2]
        P.op("y", lambda y, c=c, buf=buf:
             y.dma_start(buf[:], wo[128 * c:128 * (c + 1), :]),
             inc=lp, k=16, waits=[(CW, max(0, CW.n - 1))])
        P.op("s", lambda s_, buf=buf, kb=kb:
             s_.activation(kb[:], buf[:], AF.Copy), inc=CW,
             waits=[lp, wp])
        P.op("g", lambda g, kb=kb, c=c:
             g.dma_start(d_wo[128 * c:128 * (c + 1), :], kb[:]),
             inc=wp, k=16, waits=[CW])

    for h in range(2):
        p = [pA, pB][h]
        bank_wait(p, "t", P)
        for c in range(8):
            P.op("t", lambda t, p=p, c=c, h=h:
                 t.matmul(p[0:1, :], ones_col[:],
                          K_sb[:, 1024 * c + 512 * h:1024 * c + 512 * h + 512],
                          start=(c == 0), stop=(c == 7)),
                 inc=(GET if c == 7 else None),
                 waits=([(CW, 8), (GEN, const_v)] if c == 0 else []))
        P.op("v", lambda v, p=p, h=h:
             v.tensor_copy(cqrow[0:1, 512 * h:512 * h + 512], p[0:1, :]),
             inc=GEN, waits=[GET])
        bank_used(p, GEN)

    xqtrd = {}
    for gi in range(16):
        buf, kb = xst[gi % 2], xkbf[gi % 2]
        lp = LKP[gi % 2]
        xw = ([(CW, CW.n), (WSP[0],), (WSP[1],)] if gi < 2 else [])
        xw = [(w[0], w[0].n) if len(w) == 1 else w for w in xw]
        P.op("y", lambda y, gi=gi, buf=buf:
             y.dma_start(buf[:], xq[128 * gi:128 * (gi + 1), :]),
             inc=lp, k=16,
             waits=[(CX, max(0, CX.n - 1)), (SQ, max(0, SQ.n - 1))] + xw)
        P.op("s", lambda s_, buf=buf, kb=kb, gi=gi:
             s_.activation(kb[:], buf[:], AF.Copy, accum_out=qsums[:, gi:gi + 1]),
             inc=CX, waits=[lp, (TRD, xqtrd.get(gi - 2, 0))] + xw)
        P.op("v", lambda v, buf=buf, gi=gi:
             v.scalar_tensor_tensor(buf[:], buf[:], 1.0, buf[:], ALU.mult,
                                    ALU.mult, accum_out=qsumsq[:, gi:gi + 1]),
             inc=SQ, waits=[lp, (CX, CX.n)])
        i = gi % 4
        for half in range(2):
            p = [pC, pD][half]
            bank_wait(p, "t", P)
            for cc in range(4):
                c = 4 * half + cc
                P.op("t", lambda t, p=p, kb=kb, c=c, cc=cc:
                     t.matmul(p[:, 128 * cc:128 * (cc + 1)],
                              kb[:, 128 * c:128 * (c + 1)], ident[:],
                              start=True, stop=True),
                     inc=(TRD if cc == 3 else None),
                     waits=([CX, (LD, 64)] if cc == 0 else []))
            for eh in range(2):
                dst = ET[2 * half + eh]
                P.op("v", lambda v, p=p, dst=dst, eh=eh, i=i:
                     v.tensor_copy(
                         bass.AP(dst, 128 * i, [[1024, 128], [512, 2], [1, 128]]),
                         p[:, 256 * eh:256 * (eh + 1)]),
                     inc=TRE,
                     waits=[TRD] + ([(WS, WS.n)]
                                    if gi % 4 == 0 and eh == 0 else []))
            bank_used(p, TRE)
        xqtrd[gi] = TRD.n
        if gi % 4 == 3:
            gnum = gi // 4
            for tpart in range(4):
                P.op("g", lambda g, gnum=gnum, tpart=tpart:
                     g.dma_start(d_xqfm[:, 4096 * gnum + 1024 * tpart:
                                        4096 * gnum + 1024 * (tpart + 1)],
                                 ET[tpart][:]),
                     inc=WS, k=16, waits=[TRE])

    P.op("v", lambda v: v.tensor_scalar(t16a[:], qsums[:], 1.0 / D, None,
                                        ALU.mult), waits=[(CX, CX.n)])
    P.op("v", lambda v: v.tensor_scalar(t16b[:], qsumsq[:], 1.0 / D, None,
                                        ALU.mult), waits=[(SQ, SQ.n)])
    vfence(P)
    P.op("v", lambda v: v.tensor_tensor(t16c[:], t16a[:], t16a[:], ALU.mult))
    P.op("v", lambda v: v.tensor_scalar(munegq[:], qsums[:], -1.0 / D, None,
                                        ALU.mult))
    vfence(P)
    P.op("v", lambda v: v.tensor_tensor(t16b[:], t16b[:], t16c[:], ALU.subtract))
    P.op("v", lambda v: v.tensor_copy(munegq_bf[:], munegq[:]), inc=GEN)
    vfence(P)
    P.op("s", lambda s_: s_.activation(t16c[:], t16b[:], AF.Ln, bias=eps5[:]),
         waits=[GEN])
    sfence(P)
    P.op("s", lambda s_: s_.activation(rqs[:], t16c[:], AF.Exp, scale=-0.5),
         inc=GES)
    P.op("v", lambda v: v.tensor_scalar(rqhalf[:], rqs[:], 0.5, None, ALU.mult),
         inc=GEN, waits=[GES])
    for i in range(16):
        p = [pE, pF][i % 2]
        bank_wait(p, "t", P)
        P.op("t", lambda t, p=p, i=i:
             t.matmul(p[0:1, 0:128], munegq_bf[:, i:i + 1], ident[:],
                      start=True, stop=True), inc=GET, waits=[GEN])
        P.op("v", lambda v, p=p, i=i:
             v.tensor_copy(muq_sl(i), p[0:1, 0:128]),
             inc=GEN, waits=[GET])
        bank_used(p, GEN)

    for gnum in range(4):
        for tpart in range(4):
            P.op("y", lambda y, gnum=gnum, tpart=tpart:
                 y.dma_start(ET[tpart][:],
                             d_xqfm[:, 4096 * gnum + 1024 * tpart:
                                    4096 * gnum + 1024 * (tpart + 1)]),
                 inc=LK, k=16, waits=[(WS, WS.n), (TRE, TRE.n), (GET, GET.n)])
        for i in range(4):
            gi = 4 * gnum + i
            qtm = xkbf[gi % 2]
            for h in range(2):
                p = [pA, pB][h]
                bank_wait(p, "t", P)
                for c in range(8):
                    P.op("t", lambda t, p=p, c=c, h=h, i=i:
                         t.matmul(p[:], xqfm_sl(c, i),
                                  K_sb[:, 1024 * c + 512 * h:1024 * c + 512 * h + 512],
                                  start=(c == 0), stop=False),
                         waits=([(LK, LK.n), (GEN, GEN.n)] if c == 0 else []))
                P.op("t", lambda t, p=p, gi=gi, h=h:
                     t.matmul(p[:], muq_sl(gi),
                              cqrow[0:1, 512 * h:512 * h + 512],
                              start=False, stop=True), inc=GET)
                P.op("v", lambda v, p=p, qtm=qtm, h=h, gi=gi:
                     v.tensor_scalar(qtm[:, 512 * h:512 * h + 512], p[:],
                                     rqs[:, gi:gi + 1], None, ALU.mult),
                     inc=GEN, waits=[GET])
                bank_used(p, GEN)
            for half in range(2):
                p = [pC, pD][half]
                bank_wait(p, "t", P)
                for kk in range(4):
                    k = 4 * half + kk
                    P.op("t", lambda t, p=p, qtm=qtm, k=k, kk=kk:
                         t.matmul(p[:, 128 * kk:128 * (kk + 1)],
                                  qtm[:, 128 * k:128 * (k + 1)], ident[:],
                                  start=True, stop=True),
                         inc=(TRD if kk == 3 else None),
                         waits=([GEN] if kk == 0 else []))
                k0 = 4 * half
                P.op("v", lambda v, p=p, k0=k0, gi=gi:
                     v.tensor_copy(
                         bass.AP(Q_fm, 2048 * k0 + 128 * gi,
                                 [[16384, 128], [2048, 4], [1, 128]]),
                         p[:]),
                     inc=TRE, waits=[TRD])
                bank_used(p, TRE)
    qproj_t = GET.n
    qproj_trd = TRD.n

    ck_tsc = {}
    for c in range(8):
        buf, kb = xst[c % 2], xkbf[c % 2]
        kw = ([(SQ, SQ.n), (CX, CX.n)] if c < 2 else [])
        lp, wp = LKP[c % 2], WSP[c % 2]
        P.op("y", lambda y, c=c, buf=buf:
             y.dma_start(buf[:], wk[128 * c:128 * (c + 1), :]),
             inc=lp, k=16, waits=[(CW, max(0, CW.n - 1))] + kw)
        P.op("s", lambda s_, buf=buf, kb=kb:
             s_.activation(kb[:], buf[:], AF.Copy), inc=CW,
             waits=[lp, wp, (GET, qproj_t), (TRD, qproj_trd),
                    (TSC, ck_tsc.get(c - 2, 0))])
        for h in range(2):
            p = [pA, pB][h]
            bank_wait(p, "t", P)
            P.op("t", lambda t, p=p, kb=kb, c=c, h=h:
                 t.matmul(p[0:1, :], ones_col[:],
                          kb[:, 512 * h:512 * h + 512],
                          start=(c == 0), stop=(c == 7)),
                 inc=(GET if c == 7 else TSC), waits=[CW, (GEN, const_v)])
            if c == 7:
                P.op("v", lambda v, p=p, h=h:
                     v.tensor_copy(ckrow[0:1, 512 * h:512 * h + 512], p[0:1, :]),
                     inc=GEN, waits=[GET])
                bank_used(p, GEN)
        ck_tsc[c] = TSC.n
        P.op("g", lambda g, kb=kb, c=c:
             g.dma_start(bass.AP(d_wk.tensor, 128 * c,
                                 [[8192, 128], [1024, 8], [1, 128]]),
                         kb[:]),
             inc=wp, k=16, waits=[CW, (GET, GET.n)])
    for h in range(2):
        p = [pC, pD][h]
        bank_wait(p, "t", P)
        for c in range(8):
            P.op("t", lambda t, p=p, c=c, h=h:
                 t.matmul(p[0:1, :], ones_col[:],
                          Wv_sb[:, 1024 * c + 512 * h:1024 * c + 512 * h + 512],
                          start=(c == 0), stop=(c == 7)),
                 inc=(GET if c == 7 else None),
                 waits=([(GET, qproj_t), (GEN, const_v)] if c == 0 else []))
        P.op("v", lambda v, p=p, h=h:
             v.tensor_copy(cvrow[0:1, 512 * h:512 * h + 512], p[0:1, :]),
             inc=GEN, waits=[GET])
        bank_used(p, GEN)

    for _w in (WSP[0], WSP[1], WS):
        P.wait("g", _w)
    dump(P, [(DB.get("qstat"), qsums[:]) if dbg else None] and ([
        (DB["qstat"][0:128, 0:16], qsums[:]),
        (DB["qstat"][0:128, 16:32], qsumsq[:]),
        (DB["qstat"][0:128, 32:48], munegq[:]),
        (DB["qstat"][0:128, 48:64], rqs[:]),
        (DB["qfm"], Q_fm[:]),
        (DB["xqfmd"], d_xqfm),
        (DB["t16"][0:128, 0:16], t16a[:]),
        (DB["t16"][0:128, 16:32], t16b[:]),
        (DB["t16"][0:128, 32:48], t16c[:]),
    ] if dbg else []))
    P.run(nc)
    P = Prog()

    bq = ExitStack(); nc._kb_bq = bq

    def sa_u(u):
        return V_sb[:, 2048 * (u % 4):2048 * (u % 4) + 1024]

    def sb_u(u):
        return V_sb[:, 2048 * (u % 4) + 1024:2048 * (u % 4) + 2048]

    def b_bc_sl(lo, n):
        return Xk_fm[:, lo:lo + n]

    def lw_bc_sl(lo, n):
        return Xk_fm[:, 4096 + lo:4096 + lo + n]

    def evb_sl(eslot):
        return K_sb[:, 512 * eslot:512 * (eslot + 1)]

    def rowbuf_sl(off):
        return K_sb[0:1, 2048 + off:2048 + off + 128]

    gst = [O_acc[:, 8192 + 2048 * k:8192 + 2048 * (k + 1)] for k in range(4)]
    abs_outs = [O_acc[:, 2048 * k:2048 * (k + 1)] for k in range(4)]
    pool = [LKP[0], LKP[1], LK, LD]
    wsem = [WSP[0], WSP[1], SM["WSP2"], SM["WSP3"]]
    tbank = [pE, pF, pC, pD]
    gctr = [0]
    gparts = bq.enter_context(nc.sbuf_tensor("gparts", [128, 64], F32))
    cbparts = bq.enter_context(nc.sbuf_tensor("cbparts", [128, 64], F32))
    cons_hist = {}
    sign_hist = {}
    stt_hist = {}

    for col in range(32):
        for si, (srcsb, dstbc, off) in enumerate(
                ((lnb_bf, b_bc_sl, 0), (lnwh_bf, lw_bc_sl, 128))):
            p = [pE, pF][(2 * col + si) % 2]
            bank_wait(p, "t", P)
            P.op("t", lambda t, p=p, srcsb=srcsb, col=col:
                 t.matmul(p[0:1, 0:128], srcsb[:, col:col + 1], ident[:],
                          start=True, stop=True), inc=GET)
            P.op("v", lambda v, p=p, off=off:
                 v.tensor_copy(rowbuf_sl(off), p[0:1, 0:128]),
                 inc=GEN, waits=[GET])
            bank_used(p, GEN)
            pb = [pC, pD][(2 * col + si) % 2]
            bank_wait(pb, "t", P)
            P.op("t", lambda t, pb=pb, off=off:
                 t.matmul(pb[:, 0:128], ones_row[:], rowbuf_sl(off),
                          start=True, stop=True), inc=GET, waits=[GEN])
            P.op("v", lambda v, pb=pb, dstbc=dstbc, col=col:
                 v.tensor_copy(dstbc(128 * col, 128), pb[:, 0:128]),
                 inc=GEN, waits=[GET])
            bank_used(pb, GEN)
    bc_v = GEN.n

    for i in range(32):
        lsem = pool[i % 4]
        buf = gst[i % 4]
        if i < 16:
            ap = bass.AP(wfc.tensor, 256 * 1024 * i,
                         [[1024, 128], [128 * 1024, 2], [1, 1024]])
        else:
            j = i - 16
            ap = bass.AP(wproj.tensor, HH * 128 * (j // 2) + 2048 * (j % 2),
                         [[HH, 128], [1, 2048]])
        w = cons_hist.get(i % 4)
        P.op("y", lambda y, buf=buf, ap=ap: y.dma_start(buf, ap),
             inc=lsem, k=16, waits=([(GES, w)] if w else []))
        P.op("s", lambda s_, buf=buf, i=i:
             s_.activation(abs_outs[i % 4], buf, AF.Abs,
                           accum_out=gparts[:, i:i + 1]),
             inc=GES, waits=[lsem])
        cons_hist[i % 4] = GES.n
    ges_gamma = GES.n

    def gamma_finalize(col0, dst_p, dst_n, gslot):
        P.op("v", lambda v: v.reduce_sum(t16a[:, gslot:gslot + 1],
                                         gparts[:, col0:col0 + 16], AX.X),
             inc=GEN, waits=[(GES, ges_gamma)])
        p = pA
        bank_wait(p, "t", P)
        P.op("t", lambda t, p=p: t.matmul(p[0:1, 0:1], t16a[:, gslot:gslot + 1],
                                          ones_col32[:], start=True, stop=True),
             inc=GET, waits=[GEN])
        P.op("v", lambda v, p=p: v.tensor_scalar(g11[0:1, gslot:gslot + 1],
                                                 p[0:1, 0:1], 0.5 / (HH * D),
                                                 0.5 * Q_EPS, ALU.mult, ALU.add),
             inc=GEN, waits=[GET])
        bank_used(p, GEN)
        p2 = pB
        bank_wait(p2, "t", P)
        P.op("t", lambda t, p2=p2: t.matmul(p2[:, 0:1], ones_row32[:],
                                            g11[0:1, gslot:gslot + 1],
                                            start=True, stop=True),
             inc=GET, waits=[GEN])
        P.op("v", lambda v, p2=p2: v.tensor_copy(dst_p[:], p2[:, 0:1]),
             waits=[GET])
        P.op("v", lambda v, p2=p2: v.tensor_scalar(dst_n[:], p2[:, 0:1], -1.0,
                                                   None, ALU.mult), inc=GEN)
        bank_used(p2, GEN)

    gamma_finalize(0, taufp, taufn, 0)
    gamma_finalize(16, taupp, taupn, 1)
    tau_v = GEN.n

    for rgrp in range(16):
        lsem = pool[rgrp % 4]
        buf = gst[rgrp % 4]
        w = cons_hist.get(rgrp % 4)
        P.op("y", lambda y, rgrp=rgrp, buf=buf:
             y.dma_start(buf, bass.AP(wfc.tensor, 256 * 1024 * rgrp,
                                      [[1024, 128], [128 * 1024, 2], [1, 1024]])),
             inc=lsem, k=16, waits=([(GES, w)] if w else []))
        for rin in range(2):
            u = 2 * rgrp + rin
            src = buf[:, 1024 * rin:1024 * (rin + 1)]
            P.op("s", lambda s_, src=src, u=u:
                 s_.activation(sa_u(u), src, AF.Sign, bias=taufn[:]),
                 waits=[lsem, (GEN, tau_v), (SQ, stt_hist.get(u - 4, 0))])
            P.op("s", lambda s_, src=src, u=u:
                 s_.activation(sb_u(u), src, AF.Sign, bias=taufp[:]), inc=GES)
            sign_hist[u] = GES.n
            P.op("v", lambda v, rin=rin, u=u:
                 v.scalar_tensor_tensor(xkbf[rin][:], sa_u(u), 1.0, sb_u(u),
                                        ALU.mult, ALU.add,
                                        accum_out=cfc2_sm[:, u:u + 1]),
                 inc=SQ, waits=[(GES, sign_hist[u]), (TRD, TRD.n)]
                               + ([(WSP[rin], WSP[rin].n)] if u < 2 else []))
            stt_hist[u] = SQ.n
        cons_hist[rgrp % 4] = GES.n
        for cp in range(4):
            slot = gctr[0] % 4
            gctr[0] += 1
            p = tbank[slot]
            bank_wait(p, "t", P)
            for bi in range(4):
                ci, rin = 2 * cp + bi // 2, bi % 2
                P.op("t", lambda t, p=p, bi=bi, ci=ci, rin=rin:
                     t.matmul(p[:, 128 * bi:128 * (bi + 1)],
                              xkbf[rin][:, 128 * ci:128 * (ci + 1)], ident[:],
                              start=True, stop=True),
                     inc=(TRD if bi == 3 else None),
                     waits=([SQ] if bi == 0 else []))
            ws = wsem[slot]
            P.op("v", lambda v, p=p, slot=slot:
                 v.tensor_copy(evb_sl(slot), p[:]),
                 inc=TRE, waits=[TRD, (ws, ws.n)])
            bank_used(p, TRE)
            c0 = 2 * cp
            P.op("g", lambda g, c0=c0, rgrp=rgrp, slot=slot:
                 g.dma_start(
                     bass.AP(d_wfcq.tensor, 4096 * c0 + 256 * rgrp,
                             [[8 * HH, 128], [4096, 2], [1, 256]]),
                     evb_sl(slot)),
                 inc=ws, k=16, waits=[TRE])
    P.op("v", lambda v: v.tensor_copy(cfc2_bf[:], cfc2_sm[:]), inc=GEN,
         waits=[(SQ, SQ.n)])

    for lq in range(16):
        rr, cc2 = lq // 2, lq % 2
        lsem = pool[lq % 4]
        buf = gst[lq % 4]
        w = cons_hist.get(lq % 4)
        P.op("y", lambda y, rr=rr, cc2=cc2, buf=buf:
             y.dma_start(buf, bass.AP(wproj.tensor,
                                      HH * 128 * rr + 2048 * cc2,
                                      [[HH, 128], [1, 2048]])),
             inc=lsem, k=16, waits=([(GES, w)] if w else []))
        for half in range(2):
            cc = 2 * cc2 + half
            u = 32 + 4 * rr + cc
            seg = 1024 * cc
            src = buf[:, 1024 * half:1024 * (half + 1)]
            P.op("s", lambda s_, src=src, u=u:
                 s_.activation(sa_u(u), src, AF.Sign, bias=taupn[:]),
                 waits=[lsem, (GEN, bc_v), (SQ, stt_hist.get(u - 4, 0))])
            P.op("s", lambda s_, src=src, u=u:
                 s_.activation(sb_u(u), src, AF.Sign, bias=taupp[:]), inc=GES)
            sign_hist[u] = GES.n
            q2 = xkbf[cc % 2]
            cbc = 8 * rr + 2 * cc
            P.op("v", lambda v, q2=q2, u=u:
                 v.scalar_tensor_tensor(q2[:], sa_u(u), 1.0, sb_u(u),
                                        ALU.mult, ALU.add),
                 waits=[(GES, sign_hist[u]), (TRD, TRD.n)])
            P.op("v", lambda v, u=u, seg=seg, cbc=cbc:
                 v.scalar_tensor_tensor(ET[2][:], sa_u(u), 1.0,
                                        b_bc_sl(seg, 1024), ALU.mult, ALU.mult,
                                        accum_out=cbparts[:, cbc:cbc + 1]))
            P.op("v", lambda v, u=u, seg=seg, cbc=cbc:
                 v.scalar_tensor_tensor(K_sb[:, 4096:5120], sb_u(u), 1.0,
                                        b_bc_sl(seg, 1024), ALU.mult, ALU.mult,
                                        accum_out=cbparts[:, cbc + 1:cbc + 2]))
            vfence(P)
            P.op("v", lambda v, q2=q2, seg=seg:
                 v.tensor_tensor(ET[3][:], q2[:], lw_bc_sl(seg, 1024), ALU.mult),
                 inc=SQ)
            stt_hist[u] = SQ.n
            for hb in range(2):
                slot = gctr[0] % 4
                gctr[0] += 1
                p = tbank[slot]
                bank_wait(p, "t", P)
                for jj4 in range(4):
                    jj = 4 * hb + jj4
                    P.op("t", lambda t, p=p, jj=jj, jj4=jj4:
                         t.matmul(p[:, 128 * jj4:128 * (jj4 + 1)],
                                  ET[3][:, 128 * jj:128 * (jj + 1)], ident[:],
                                  start=True, stop=True),
                         inc=(TRD if jj4 == 3 else None),
                         waits=([SQ] if jj4 == 0 else []))
                ws = wsem[slot]
                P.op("v", lambda v, p=p, slot=slot:
                     v.tensor_copy(evb_sl(slot), p[:]),
                     inc=TRE, waits=[TRD, (ws, ws.n)])
                bank_used(p, TRE)
                jj0 = 4 * hb
                P.op("g", lambda g, cc=cc, jj0=jj0, rr=rr, slot=slot:
                     g.dma_start(
                         bass.AP(d_wprojq.tensor,
                                 1024 * (8 * cc + jj0) + 512 * (rr // 4)
                                 + 128 * (rr % 4),
                                 [[32 * 1024, 128], [1024, 4], [1, 128]]),
                         evb_sl(slot)),
                     inc=ws, k=16, waits=[TRE])
        cons_hist[lq % 4] = GES.n
    for rr in range(8):
        P.op("v", lambda v, rr=rr:
             v.reduce_sum(cb8[:, rr:rr + 1], cbparts[:, 8 * rr:8 * rr + 8], AX.X),
             waits=[(SQ, SQ.n)])
    vfence(P)
    vfence(P)
    P.op("v", lambda v: v.tensor_copy(cb8_bf[:], cb8[:]), inc=GEN)

    for _w in wsem:
        P.wait("g", _w)
    b05w = [(w, w.n) for w in wsem]
    dump(P, ([
        (DB["tau"][0:128, 0:1], taufp[:]), (DB["tau"][0:128, 1:2], taupp[:]),
        (DB["cfc"], cfc2_sm[:]), (DB["cb"], cb8[:]),
        (DB["wfcq"], d_wfcq), (DB["wprojq"], d_wprojq),
    ] if dbg else []))
    P.run(nc)
    P = Prog()
    bq.close()

    def ETsl(tc, lo, n):
        return ET[tc // 2][:, 512 * (tc % 2) + lo: 512 * (tc % 2) + lo + n]

    nslots = [0]

    def xk_prep(s):
        par = s % 2
        for u in range(8):
            buf, kb = xst[u % 2], xkbf[u % 4]
            lp = LKP[u % 2]
            P.op("y", lambda y, s=s, u=u, buf=buf:
                 y.dma_start(buf[:],
                             xk[1024 * s + 128 * u:1024 * s + 128 * (u + 1), :]),
                 inc=lp, k=16,
                 waits=[(CX, max(0, CX.n - 1)), (SQ, max(0, SQ.n - 1))])
            P.op("s", lambda s_, kb=kb, buf=buf, u=u:
                 s_.activation(kb[:], buf[:], AF.Copy,
                               accum_out=ksums[:, u:u + 1]),
                 inc=CX, waits=[lp, (TRD, max(0, TRD.n - 4))])
            P.op("v", lambda v, buf=buf, u=u:
                 v.scalar_tensor_tensor(buf[:], buf[:], 1.0, buf[:], ALU.mult,
                                        ALU.mult, accum_out=ksumsq[:, u:u + 1]),
                 inc=SQ, waits=[lp, (CX, CX.n)])
            for half in range(2):
                p = [pE, pF, pC, pD][(2 * u + half) % 4]
                bank_wait(p, "t", P)
                for cc in range(4):
                    c = 4 * half + cc
                    P.op("t", lambda t, p=p, kb=kb, c=c, cc=cc:
                         t.matmul(p[:, 128 * cc:128 * (cc + 1)],
                                  kb[:, 128 * c:128 * (c + 1)], ident[:],
                                  start=True, stop=True),
                         inc=(TRD if cc == 3 else None),
                         waits=([CX] if cc == 0 else []))
                c0 = 4 * half
                P.op("v", lambda v, p=p, c0=c0, u=u:
                     v.tensor_copy(
                         bass.AP(Xk_fm, 1024 * c0 + 128 * u,
                                 [[8192, 128], [1024, 4], [1, 128]]),
                         p[:]),
                     inc=TRE, waits=[TRD])
                bank_used(p, TRE)
        vfence(P)
        P.op("v", lambda v: v.tensor_scalar(t8a[:], ksums[:], 1.0 / D, None,
                                            ALU.mult), waits=[(CX, CX.n)])
        P.op("v", lambda v: v.tensor_scalar(t8b[:], ksumsq[:], 1.0 / D, None,
                                            ALU.mult), waits=[(SQ, SQ.n)])
        P.op("v", lambda v: v.tensor_scalar(munegk[:], ksums[:], -1.0 / D,
                                            None, ALU.mult))
        vfence(P)
        P.op("v", lambda v: v.tensor_tensor(t8a[:], t8a[:], t8a[:], ALU.mult))
        P.op("v", lambda v: v.tensor_copy(munegk_bf[:], munegk[:]))
        vfence(P)
        P.op("v", lambda v: v.tensor_tensor(t8b[:], t8b[:], t8a[:], ALU.subtract),
             inc=GEN)
        vfence(P)
        P.op("s", lambda s_: s_.activation(t8a[:], t8b[:], AF.Ln, bias=eps5[:]),
             waits=[GEN])
        sfence(P)
        P.op("s", lambda s_, par=par:
             s_.activation(rks[par][:], t8a[:], AF.Exp, scale=-0.5), inc=GES)
        P.op("v", lambda v, par=par:
             v.tensor_scalar(rk32[par][:], rks[par][:], 1.0 / 32, None, ALU.mult),
             inc=GEN, waits=[GES])
        for tc in range(8):
            p = [pE, pF, pC, pD][tc % 4]
            bank_wait(p, "t", P)
            P.op("t", lambda t, p=p, tc=tc:
                 t.matmul(p[0:1, 0:128], munegk_bf[:, tc:tc + 1], ident[:],
                          start=True, stop=True), inc=GET, waits=[GEN])
            P.op("v", lambda v, p=p, tc=tc:
                 v.tensor_copy(mukflat[0:1, 128 * tc:128 * (tc + 1)],
                               p[0:1, 0:128]),
                 inc=(STA if tc == 7 else GEN), waits=[GET])
            bank_used(p, STA if tc == 7 else GEN)

    P.op("v", lambda v: v.memset(O_acc[:], 0.0), inc=GEN)
    P.op("v", lambda v: v.memset(Z_acc[:], 0.0), inc=GEN)
    xk_prep(0)
    kv_banks = [pA, pB, pC, pD]
    wk_hist = [0, 0]

    for s in range(NSWEEP):
        par = s % 2
        active = [j for j in range(4) if SLOT_SWEEPS[j] > s]
        bk = 0
        for k in range(8):
            wb = wkst[k % 2]
            lg = LKG[k % 2]
            P.op("g", lambda g, wb=wb, k=k:
                 g.dma_start(wb[:], d_wk[:, 1024 * k:1024 * (k + 1)]),
                 inc=lg, k=16, waits=[(GET, wk_hist[-2])])
            wk_ld = (lg, lg.n)
            for h in range(2):
                p = kv_banks[bk % 4]; bk += 1
                bank_wait(p, "t", P)
                for c in range(8):
                    P.op("t", lambda t, p=p, c=c, wb=wb, h=h:
                         t.matmul(p[:],
                                  wb[:, 128 * c:128 * (c + 1)],
                                  Xk_fm[:, 1024 * c + 512 * h:1024 * c + 512 * h + 512],
                                  start=(c == 0), stop=False),
                         waits=([(TRE, TRE.n), (STA, s + 1), wk_ld]
                                if c == 0 else []))
                P.op("t", lambda t, p=p, k=k, h=h:
                     t.matmul(p[:], ckrow[0:1, 128 * k:128 * (k + 1)],
                              mukflat[0:1, 512 * h:512 * h + 512],
                              start=False, stop=True), inc=GET)
                P.op("s", lambda s_, p=p, k=k, h=h:
                     s_.activation(K_sb[:, 1024 * k + 512 * h:1024 * k + 512 * h + 512],
                                   p[:], AF.Copy),
                     inc=KVE, waits=[GET, (TAV, 8 * nslots[0])] + b05w)
                bank_used(p, KVE)
            wk_hist.append(GET.n)
        for tc in range(8):
            for h in range(2):
                p = kv_banks[bk % 4]; bk += 1
                bank_wait(p, "t", P)
                for c in range(8):
                    P.op("t", lambda t, p=p, c=c, tc=tc, h=h:
                         t.matmul(p[:],
                                  Xk_fm[:, 1024 * c + 128 * tc:1024 * c + 128 * (tc + 1)],
                                  Wv_sb[:, 1024 * c + 512 * h:1024 * c + 512 * h + 512],
                                  start=(c == 0), stop=False))
                P.op("t", lambda t, p=p, tc=tc, h=h:
                     t.matmul(p[:], mukflat[0:1, 128 * tc:128 * (tc + 1)],
                              cvrow[0:1, 512 * h:512 * h + 512],
                              start=False, stop=True), inc=GET)
                P.op("s", lambda s_, p=p, tc=tc, h=h, par=par:
                     s_.activation(V_sb[:, 1024 * tc + 512 * h:1024 * tc + 512 * h + 512],
                                   p[:], AF.Identity, scale=rks[par][:, tc:tc + 1]),
                     inc=KVE, waits=[GET])
                bank_used(p, KVE)

        for si, j in enumerate(active):
            masked = s >= SLOT_SWEEPS[j] - 2
            for tc in range(8):
                p = [pA, pB][tc % 2]
                bank_wait(p, "t", P)
                for k in range(8):
                    P.op("t", lambda t, p=p, k=k, tc=tc, j=j:
                         t.matmul(p[:],
                                  K_sb[:, 1024 * k + 128 * tc:1024 * k + 128 * (tc + 1)],
                                  Q_fm[:, 2048 * k + 512 * j:2048 * k + 512 * (j + 1)],
                                  start=(k == 0), stop=(k == 7)),
                         inc=(TSC if k == 7 else None),
                         waits=([(KVE, KVE.n)] if k == 0 else []))
                P.op("s", lambda s_, p=p, tc=tc, par=par:
                     s_.activation(ETsl(tc, 0, 512), p[:], AF.Exp,
                                   scale=rk32[par][:, tc:tc + 1]),
                     inc=SCE, waits=[TSC, (TAV, 8 * nslots[0])])
                bank_used(p, SCE)
                if masked:
                    col = 16 * j + 8 * (s - (SLOT_SWEEPS[j] - 2)) + tc
                    cs = cmp_t[:, 512 * (tc % 2):512 * (tc % 2) + 512]
                    P.op("v", lambda v, col=col, cs=cs:
                         v.tensor_single_scalar(cs, iq[:],
                                                thr_sb[:, col:col + 1], ALU.is_ge))
                    vfence(P)
                    P.op("v", lambda v, tc=tc, cs=cs:
                         v.tensor_tensor(ETsl(tc, 0, 512), ETsl(tc, 0, 512),
                                         cs, ALU.mult),
                         inc=MSK, waits=[SCE])
            pz = [pz0, pz1][nslots[0] % 2]
            bank_wait(pz, "t", P)
            for i in range(4):
                for tc in range(8):
                    P.op("t", lambda t, pz=pz, i=i, tc=tc:
                         t.matmul(pz[:, i:i + 1], ETsl(tc, 128 * i, 128),
                                  ones_col[:], start=(tc == 0), stop=(tc == 7)),
                         inc=(TZ if (i == 3 and tc == 7) else None),
                         waits=([(SCE, SCE.n)] + ([(MSK, MSK.n)] if masked else [])
                                if (i == 0 and tc == 0) else []))
            P.op("v", lambda v, pz=pz, j=j:
                 v.tensor_tensor(Z_acc[:, 4 * j:4 * j + 4], pz[:, 0:4],
                                 Z_acc[:, 4 * j:4 * j + 4], ALU.add),
                 inc=ZAE, waits=[TZ])
            bank_used(pz, ZAE)
            for h in range(2):
                for i in range(4):
                    gi = 4 * j + i
                    p = [pC, pD][i % 2]
                    bank_wait(p, "t", P)
                    for tc in range(8):
                        P.op("t", lambda t, p=p, i=i, tc=tc, h=h:
                             t.matmul(p[:], ETsl(tc, 128 * i, 128),
                                      V_sb[:, 1024 * tc + 512 * h:1024 * tc + 512 * h + 512],
                                      start=(tc == 0), stop=(tc == 7)),
                             inc=(TAV if tc == 7 else None))
                    P.op("v", lambda v, p=p, gi=gi, h=h:
                         v.tensor_tensor(
                             O_acc[:, 1024 * gi + 512 * h:1024 * gi + 512 * h + 512],
                             p[:],
                             O_acc[:, 1024 * gi + 512 * h:1024 * gi + 512 * h + 512],
                             ALU.add),
                         inc=AVE, waits=[TAV])
                    bank_used(p, AVE)
            nslots[0] += 1
            if si == 0 and s + 1 < NSWEEP:
                xk_prep(s + 1)
        if dbg:
            P.op("g", lambda g, s=s:
                 g.dma_start(DB["Zs"][s], Z_acc[:]),
                 inc=DBG, k=16, waits=[(ZAE, ZAE.n), (AVE, AVE.n)])
            P.op("g", lambda g, s=s:
                 g.dma_start(DB["Os"][s], O_acc[:, 0:1024]),
                 inc=DBG, k=16)

    dump(P, ([
        (DB["Z"], Z_acc[:]), (DB["O"], O_acc[:]),
        (DB["ks"][0:128, 0:8], ksums[:]), (DB["ks"][0:128, 8:16], ksumsq[:]),
        (DB["ks"][0:128, 16:24], rks[0][:]), (DB["ks"][0:128, 24:32], rks[1][:]),
        (DB["ks"][0:128, 32:40], rk32[0][:]), (DB["ks"][0:128, 40:48], rk32[1][:]),
        (DB["ksb"], K_sb[:]), (DB["vsb"], V_sb[:]), (DB["xkfm"], Xk_fm[:]),
    ] if dbg else []))
    P.run(nc)
    P = Prog()
    a1.close()

    b2s = ExitStack(); nc._kb_b2s = b2s
    sout = [b2s.enter_context(nc.sbuf_tensor(f"sout{i}", [128, 1024], BF16))
            for i in range(2)]
    for c in range(8):
        P.op("y", lambda y, c=c:
             y.dma_start(K_sb[:, 1024 * c:1024 * (c + 1)],
                         d_wo[128 * c:128 * (c + 1), :]),
             inc=LD, k=16)
    wo_ld = LD.n
    P.op("v", lambda v: v.reciprocal(Zrec[:], Z_acc[:]), inc=GEN)
    for gi in range(16):
        obf, ofm = ET[gi % 2], ET[2 + gi % 2]
        P.op("s", lambda s_, obf=obf, gi=gi:
             s_.activation(obf[:], O_acc[:, 1024 * gi:1024 * (gi + 1)], AF.Copy),
             inc=GES, waits=[(GET, GET.n)])
        for half in range(2):
            p = [pE, pF, pC, pD][(2 * gi + half) % 4]
            bank_wait(p, "t", P)
            for cc in range(4):
                c = 4 * half + cc
                P.op("t", lambda t, p=p, obf=obf, c=c, cc=cc:
                     t.matmul(p[:, 128 * cc:128 * (cc + 1)],
                              obf[:, 128 * c:128 * (c + 1)], ident[:],
                              start=True, stop=True),
                     inc=(TRD if cc == 3 else None),
                     waits=([GES] if cc == 0 else []))
            c0 = 4 * half
            P.op("v", lambda v, p=p, ofm=ofm, c0=c0:
                 v.tensor_copy(ofm[:, 128 * c0:128 * (c0 + 4)], p[:]),
                 inc=TRE, waits=[TRD])
            bank_used(p, TRE)
        for h in range(2):
            p = [pA, pB][h]
            bank_wait(p, "t", P)
            for c in range(8):
                P.op("t", lambda t, p=p, ofm=ofm, c=c, h=h:
                     t.matmul(p[:], ofm[:, 128 * c:128 * (c + 1)],
                              K_sb[:, 1024 * c + 512 * h:1024 * c + 512 * h + 512],
                              start=(c == 0), stop=(c == 7)),
                     inc=(GET if c == 7 else None),
                     waits=([(TRE, TRE.n), (LD, wo_ld)] if c == 0 else []))
            P.op("v", lambda v, p=p, gi=gi, h=h:
                 v.tensor_scalar(sout[gi % 2][:, 512 * h:512 * h + 512], p[:],
                                 Zrec[:, gi:gi + 1], None, ALU.mult),
                 inc=GEN, waits=[GET, (WSP[gi % 2], WSP[gi % 2].n)])
            bank_used(p, GEN)
        P.op("g", lambda g, gi=gi:
             g.dma_start(d_s[128 * gi:128 * (gi + 1), :], sout[gi % 2][:]),
             inc=WSP[gi % 2], k=16, waits=[GEN])

    for _w in (WSP[0], WSP[1]):
        P.wait("g", _w)
    dump(P, ([(DB["sdump"], d_s)] if dbg else []))
    P.run(nc)
    P = Prog()
    b2s.close()
    a2.close()

    f1 = ExitStack(); nc._kb_f1 = f1
    ef = f1.enter_context
    wfc_sb = ef(nc.sbuf_tensor("wfc_sb", [128, 8 * HH], BF16))
    xqg = [ef(nc.sbuf_tensor(f"xqg{i}", [128, 4096], BF16)) for i in range(2)]
    hg = ef(nc.sbuf_tensor("hg", [128, HH], F32))
    hh_bf = ef(nc.sbuf_tensor("hh_bf", [128, HH], BF16))
    sqjunk = ef(nc.sbuf_tensor("sqjunk", [128, HH], BF16))
    hfm = ef(nc.sbuf_tensor("hfm", [128, 32 * 512], BF16))
    zt = [ef(nc.sbuf_tensor(f"zt{i}", [128, 1024], F32)) for i in range(4)]
    wprojst = [ef(nc.sbuf_tensor(f"wpst{i}", [128, 2048], BF16)) for i in range(2)]
    cfc2fl = ef(nc.sbuf_tensor("cfc2fl", [1, HH], BF16))
    muqfl = ef(nc.sbuf_tensor("muqfl", [1, 2048], BF16))
    h1 = ef(nc.sbuf_tensor("h1", [128, 1], F32))
    h2 = ef(nc.sbuf_tensor("h2", [128, 1], F32))
    h3 = ef(nc.sbuf_tensor("h3", [128, 1], F32))

    for i in range(8):
        P.op("y", lambda y, i=i:
             y.dma_start(wfc_sb[:, 4096 * i:4096 * (i + 1)],
                         d_wfcq[:, 4096 * i:4096 * (i + 1)]),
             inc=LD, k=16)
    wfc_ld = LD.n
    for i in range(32):
        p = [pE, pF][i % 2]
        bank_wait(p, "t", P)
        P.op("t", lambda t, p=p, i=i:
             t.matmul(p[0:1, 0:128], cfc2_bf[:, i:i + 1], ident[:],
                      start=True, stop=True), inc=GET)
        P.op("v", lambda v, p=p, i=i:
             v.tensor_copy(cfc2fl[0:1, 128 * i:128 * (i + 1)], p[0:1, 0:128]),
             inc=GEN, waits=[GET])
        bank_used(p, GEN)
    for i in range(16):
        p = [pE, pF][i % 2]
        bank_wait(p, "t", P)
        P.op("t", lambda t, p=p, i=i:
             t.matmul(p[0:1, 0:128], munegq_bf[:, i:i + 1], ident[:],
                      start=True, stop=True), inc=GET)
        P.op("v", lambda v, p=p, i=i:
             v.tensor_copy(muqfl[0:1, 128 * i:128 * (i + 1)], p[0:1, 0:128]),
             inc=GEN, waits=[GET])
        bank_used(p, GEN)
    rows_v, rows_t = GEN.n, GET.n

    fc_t = {}
    jj_hist = [0, 0]
    center_v = {}

    def sched_fc(t):
        gnum, i = t // 4, t % 4
        gi = t
        hcol, scol = 16 * (t % 2), 16 * (t % 2) + 8
        lpg = LKP[gnum % 2]
        if i == 0:
            P.op("y", lambda y, gnum=gnum:
                 y.dma_start(xqg[gnum % 2][:],
                             d_xqfm[:, 4096 * gnum:4096 * (gnum + 1)]),
                 inc=lpg, k=16, waits=[(GET, fc_t.get(gnum - 2, 0))])
        for seg in range(8):
            p = [pA, pB][seg % 2]
            bank_wait(p, "t", P)
            for c in range(8):
                P.op("t", lambda t_, p=p, c=c, seg=seg, i=i, gnum=gnum:
                     t_.matmul(p[:],
                               xqg[gnum % 2][:, 512 * c + 128 * i:
                                             512 * c + 128 * (i + 1)],
                               wfc_sb[:, 4096 * c + 512 * seg:
                                      4096 * c + 512 * (seg + 1)],
                               start=(c == 0), stop=False),
                     waits=([lpg, (LD, wfc_ld), (GEN, rows_v)]
                            if c == 0 else []))
            P.op("t", lambda t_, p=p, gi=gi, seg=seg:
                 t_.matmul(p[:], muqfl[0:1, 128 * gi:128 * (gi + 1)],
                           cfc2fl[0:1, 512 * seg:512 * (seg + 1)],
                           start=False, stop=True), inc=GET)
            P.op("s", lambda s_, p=p, seg=seg, gi=gi, hcol=hcol:
                 s_.activation(hg[:, 512 * seg:512 * (seg + 1)], p[:],
                               AF.Gelu, scale=rqhalf[:, gi:gi + 1],
                               accum_out=hsums[:, hcol + seg:hcol + seg + 1]),
                 inc=GES, waits=[GET, (GEN, center_v.get(t - 1, 0))])
            bank_used(p, GES)
            P.op("v", lambda v, seg=seg, scol=scol:
                 v.scalar_tensor_tensor(sqjunk[:, 512 * seg:512 * (seg + 1)],
                                        hg[:, 512 * seg:512 * (seg + 1)], 1.0,
                                        hg[:, 512 * seg:512 * (seg + 1)],
                                        ALU.mult, ALU.mult,
                                        accum_out=hsums[:, scol + seg:
                                                        scol + seg + 1]),
                 waits=[(GES, GES.n)])
        if i == 3:
            fc_t[gnum] = GET.n

    def sched_stats(t):
        gi = t
        hcol, scol = 16 * (t % 2), 16 * (t % 2) + 8
        P.op("v", lambda v, hcol=hcol:
             v.reduce_sum(h1[:], hsums[:, hcol:hcol + 8], AX.X),
             waits=[(GES, GES.n)])
        vfence(P)
        P.op("v", lambda v: v.tensor_scalar(h2[:], h1[:], -1.0 / HH, None,
                                            ALU.mult))
        vfence(P)
        P.op("v", lambda v:
             v.tensor_scalar(hh_bf[:], hg[:], h2[:], None, ALU.add),
             inc=GEN, waits=[(TRD, TRD.n)])
        center_v[t] = GEN.n
        P.op("v", lambda v, scol=scol:
             v.reduce_sum(h3[:], hsums[:, scol:scol + 8], AX.X))
        vfence(P)
        P.op("v", lambda v: v.tensor_scalar(t16a[:, 0:1], h3[:], 1.0 / HH,
                                            None, ALU.mult))
        P.op("v", lambda v: v.tensor_tensor(t16b[:, 0:1], h2[:], h2[:],
                                            ALU.mult))
        vfence(P)
        P.op("v", lambda v, gi=gi:
             v.tensor_tensor(varh[:, gi:gi + 1], t16a[:, 0:1], t16b[:, 0:1],
                             ALU.subtract), inc=GEN)

    def sched_transposes(t):
        i = t % 4
        for bb in range(8):
            p = [pC, pD][bb % 2]
            bank_wait(p, "t", P)
            for jj4 in range(4):
                jj = 4 * bb + jj4
                P.op("t", lambda t_, p=p, jj=jj, jj4=jj4:
                     t_.matmul(p[:, 128 * jj4:128 * (jj4 + 1)],
                               hh_bf[:, 128 * jj:128 * (jj + 1)], ident[:],
                               start=True, stop=True),
                     inc=(TRD if jj4 == 3 else None),
                     waits=([(GEN, center_v[t])] if jj4 == 0 else []))
            jj0 = 4 * bb
            P.op("v", lambda v, p=p, jj0=jj0, i=i:
                 v.tensor_copy(
                     bass.AP(hfm, 512 * jj0 + 128 * i,
                             [[32 * 512, 128], [512, 4], [1, 128]]),
                     p[:]),
                 inc=TRE, waits=[TRD])
            bank_used(p, TRE)

    def sched_proj(gnum):
        for h in range(2):
            pbanks = [pA, pB, pE, pF]
            for p in pbanks:
                bank_wait(p, "t", P)
            for jg in range(8):
                wb = wprojst[jg % 2]
                lgj = LKJ[jg % 2]
                P.op("y", lambda y, wb=wb, jg=jg, h=h:
                     y.dma_start(wb[:],
                                  bass.AP(d_wprojq.tensor,
                                          1024 * 4 * jg + 512 * h,
                                          [[32 * 1024, 128], [1024, 4], [1, 512]])),
                     inc=lgj, k=16, waits=[(GET, jj_hist[-2])])
                for jl in range(4):
                    jjj = 4 * jg + jl
                    for i in range(4):
                        P.op("t", lambda t_, p=pbanks[i], wb=wb, jjj=jjj, jl=jl, i=i:
                             t_.matmul(p[:],
                                      hfm[:, 512 * jjj + 128 * i:
                                          512 * jjj + 128 * (i + 1)],
                                      wb[:, 512 * jl:512 * (jl + 1)],
                                      start=(jjj == 0), stop=(jjj == 31)),
                             inc=(GET if i == 3 and jl == 3 else None),
                             waits=(([lgj] + ([(TRE, TRE.n)] if jjj == 0 else []))
                                    if i == 0 and jl == 0 else []))
                jj_hist.append(GET.n)
            for i in range(4):
                P.op("s", lambda s_, p=pbanks[i], i=i, h=h:
                     s_.activation(zt[i][:, 512 * h:512 * h + 512], p[:], AF.Copy),
                     inc=GES, waits=[(GET, GET.n), (WS, WS.n)])
                bank_used(pbanks[i], GES)
        for i in range(4):
            gi = 4 * gnum + i
            P.op("g", lambda g, i=i, gi=gi:
                 g.dma_start(d_z[128 * gi:128 * (gi + 1), :], zt[i][:]),
                 inc=WS, k=16, waits=[(GES, GES.n)])

    for t in range(17):
        if t < 16:
            sched_fc(t)
        if t > 0:
            sched_transposes(t - 1)
        if t < 16:
            sched_stats(t)
        if t > 0 and t % 4 == 0:
            sched_proj(t // 4 - 1)

    P.wait("g", WS)
    dump(P, ([(DB["zdump"], d_z), (DB["varh"], varh[:]),
              (DB["hg15"], hg[:]), (DB["hh15"], hh_bf[:]),
              (DB["hfm15"], hfm[:])] if dbg else []))
    P.run(nc)
    P = Prog()

    rh = ef(nc.sbuf_tensor("rh", [128, 16], F32))
    cb_bc = ef(nc.sbuf_tensor("cb_bc", [128, 1024], F32))
    outb = [ef(nc.sbuf_tensor(f"outb{i}", [128, 1024], F32)) for i in range(2)]

    def cb_fl_sl(i):
        return sqjunk[0:1, 128 * i:128 * (i + 1)]

    zin = [zt[0][:], zt[1][:]]
    zbuf = [zt[2][:], zt[3][:]]
    sin = [hh_bf[:, 0:1024], hh_bf[:, 1024:2048]]
    gbuf = [hh_bf[:, 2048:3072], hh_bf[:, 3072:4096]]
    xin = [hg[:, 0:1024], hg[:, 1024:2048]]
    tbuf = [hg[:, 2048:3072], hg[:, 3072:4096]]

    P.op("s", lambda s_: s_.activation(t16a[:], varh[:], AF.Sqrt, bias=eps5[:]),
         inc=GES)
    P.op("v", lambda v: v.reciprocal(t16b[:], t16a[:]), waits=[(GES, GES.n)])
    P.op("v", lambda v: v.tensor_scalar(t16c[:], varh[:], 1.0, LN_EPS, ALU.mult,
                                        ALU.add))
    vfence(P)
    P.op("v", lambda v: v.tensor_tensor(rh[:], t16b[:], t16b[:], ALU.mult))
    vfence(P)
    P.op("v", lambda v: v.tensor_tensor(rh[:], rh[:], t16c[:], ALU.mult))
    vfence(P)
    P.op("v", lambda v: v.tensor_scalar(rh[:], rh[:], -0.5, 1.5, ALU.mult,
                                        ALU.add))
    vfence(P)
    P.op("v", lambda v: v.tensor_tensor(rh[:], rh[:], t16b[:], ALU.mult), inc=GEN)

    for i in range(8):
        p = [pE, pF][i % 2]
        bank_wait(p, "t", P)
        P.op("t", lambda t, p=p, i=i:
             t.matmul(p[0:1, 0:128], cb8_bf[:, i:i + 1], ident[:],
                      start=True, stop=True), inc=GET)
        P.op("v", lambda v, p=p, i=i:
             v.tensor_copy(cb_fl_sl(i), p[0:1, 0:128]),
             inc=GEN, waits=[GET])
        bank_used(p, GEN)
    for i in range(8):
        p = [pA, pB][i % 2]
        bank_wait(p, "t", P)
        P.op("t", lambda t, p=p, i=i:
             t.matmul(p[:, 0:128], ones_row[:], cb_fl_sl(i),
                      start=True, stop=True), inc=GET, waits=[(GEN, GEN.n)])
        P.op("v", lambda v, p=p, i=i:
             v.tensor_scalar(cb_bc[:, 128 * i:128 * (i + 1)], p[:, 0:128],
                             0.5, None, ALU.mult),
             inc=GEN, waits=[GET])
        bank_used(p, GEN)
    prep_v = GEN.n

    tile_v = {}
    for gi in range(16):
        par = gi % 2
        lp = LKP[par]
        P.op("y", lambda y, gi=gi, par=par:
             y.dma_start(zin[par], d_z[128 * gi:128 * (gi + 1), :]),
             inc=lp, k=16, waits=[(GEN, tile_v.get(gi - 2, 0))])
        P.op("s", lambda s_, gi=gi, par=par:
             s_.dma_start(sin[par], d_s[128 * gi:128 * (gi + 1), :]),
             inc=lp, k=16, waits=[(GEN, tile_v.get(gi - 2, 0))])
        P.op("y", lambda y, gi=gi, par=par:
             y.dma_start(xin[par], xq[128 * gi:128 * (gi + 1), :]),
             inc=lp, k=16)
        P.op("v", lambda v, gi=gi, par=par:
             v.scalar_tensor_tensor(zbuf[par], zin[par], rh[:, gi:gi + 1],
                                    cb_bc[:], ALU.mult, ALU.add),
             inc=GEN, waits=[(lp, lp.n), (GEN, prep_v), (GES, GES.n)])
        P.op("s", lambda s_, par=par:
             s_.activation(gbuf[par], zbuf[par], AF.Silu),
             inc=GES, waits=[(GEN, GEN.n)])
        P.op("v", lambda v, par=par:
             v.tensor_tensor(tbuf[par], gbuf[par], sin[par], ALU.mult),
             inc=GEN, waits=[(GES, GES.n)])
        vfence(P)
        P.op("v", lambda v, par=par:
             v.tensor_tensor(outb[par][:], tbuf[par], xin[par], ALU.add),
             inc=GEN, waits=[(OUT, OUT.n)])
        tile_v[gi] = GEN.n
        P.op("g", lambda g, gi=gi, par=par:
             g.dma_start(out[128 * gi:128 * (gi + 1), :], outb[par][:]),
             inc=OUT, k=16, waits=[(GEN, GEN.n)])

    nout = OUT.n
    ndbg = DBG.n
    P.op("g", lambda g: g.wait_ge(OUT.h, nout))
    if ndbg:
        P.op("g", lambda g: g.wait_ge(DBG.h, ndbg))
    dump(P, ([(DB["rh"], rh[:]),
              (DB["cbbc"], cb_bc[:]),
              (DB["cbfl"], sqjunk[0:1, 0:1024]),
              (DB["nt"][0:128, 0:16], t16a[:]),
              (DB["nt"][0:128, 16:32], t16b[:]),
              (DB["nt"][0:128, 32:48], t16c[:])] if dbg else []))
    P.run(nc)

    f1.close()
    ctx.close()
    return nc


def _pos(m):
    return [12 + m, 8 + m, 4 + m, m]


def _host_inputs(x, Wq, Wk, Wv, Wo, W_fc, ln_w, ln_b, W_proj):
    bf = ml_dtypes.bfloat16
    ident = np.eye(128, dtype=np.float32).astype(bf)
    lnw_t = np.ascontiguousarray(ln_w.reshape(32, 128).T)
    lnb_t = np.ascontiguousarray(ln_b.reshape(32, 128).T)
    shared = {"wq": Wq, "wk": Wk, "wv": Wv, "wo": Wo, "wfc": W_fc,
              "wproj": W_proj, "lnw_t": lnw_t, "lnb_t": lnb_t,
              "ident_in": ident}
    in_maps = []
    p_arr = np.arange(128)
    for c in range(8):
        b, m = c // 4, c % 4
        pos = _pos(m)
        xkc = np.ascontiguousarray(x[b])
        xqc = np.ascontiguousarray(
            np.concatenate([xkc[512 * p:512 * (p + 1)] for p in pos], axis=0))
        thr = np.zeros((128, 64), dtype=np.float32)
        for j in range(4):
            for srel in range(2):
                s_abs = SLOT_SWEEPS[j] - 2 + srel
                for tc in range(8):
                    col = 16 * j + 8 * srel + tc
                    thr[:, col] = (1024 * s_abs + 128 * tc + p_arr
                                   - 512 * pos[j])
        im = dict(shared)
        im["xk"] = xkc
        im["xq"] = xqc
        im["thr_in"] = thr
        in_maps.append(im)
    return in_maps


def kernel(L0_attn_x, L0_attn_Wq, L0_attn_Wk, L0_attn_Wv, L0_attn_Wo,
           L0_ffn_W_fc, L0_ffn_ln_w, L0_ffn_ln_b, L0_ffn_W_proj):
    x = np.asarray(L0_attn_x, dtype=np.float32)
    in_maps = _host_inputs(
        x,
        np.ascontiguousarray(np.asarray(L0_attn_Wq, np.float32)),
        np.ascontiguousarray(np.asarray(L0_attn_Wk, np.float32)),
        np.ascontiguousarray(np.asarray(L0_attn_Wv, np.float32)),
        np.ascontiguousarray(np.asarray(L0_attn_Wo, np.float32)),
        np.ascontiguousarray(np.asarray(L0_ffn_W_fc, np.float32)),
        np.ascontiguousarray(np.asarray(L0_ffn_ln_w, np.float32)),
        np.ascontiguousarray(np.asarray(L0_ffn_ln_b, np.float32)),
        np.ascontiguousarray(np.asarray(L0_ffn_W_proj, np.float32)))
    nc = build_nc()
    res = run_bass_kernel_spmd(nc, in_maps, core_ids=list(range(8)))
    outa = np.empty((2, 8192, 1024), np.float32)
    for c in range(8):
        b, m = c // 4, c % 4
        o = res.results[c]["out"]
        for j, p in enumerate(_pos(m)):
            outa[b, 512 * p:512 * (p + 1)] = o[512 * j:512 * (j + 1)]
    return outa
